# version 07
# baseline (speedup 1.0000x reference)
import numpy as np
import ml_dtypes
from contextlib import ExitStack

import concourse.bass as bass
import concourse.mybir as mybir
from concourse.bass_utils import run_bass_kernel_spmd

F32 = mybir.dt.float32
BF16 = mybir.dt.bfloat16
I32 = mybir.dt.int32
AF = mybir.ActivationFunctionType
ALU = mybir.AluOpType

D = 2048
HCN = 16
CI = 512
CCN = 4
II = 8192
KIN = 64
NQ = 4
LEXT = 2048
LOWN = 1024
BLK = 512
NBLK = 4
DTR = 128
NH = 32

SBUF_BASE = 16448
SBUF_TOP = 229376 - 128

DMA_ENGS = ("sync", "gp")
PHYS = {"sync": "sync", "gp": "gpsimd", "gpc": "gpsimd", "tensor": "tensor",
        "vector": "vector", "scalar": "scalar"}
DMA_SEM_NAMES = {f"{q}{j}" for q in ("sync", "gp") for j in range(64)}

ALOG, CONVW, CONVB, DVEC, BDT, PMASK, GSC, USC = 0, 4, 20, 24, 28, 29, 30, 94


NDS = 8


class Builder:

    def __init__(self):
        self.ops = []
        self.awaited = set()

    def emit(self, eng, fn, waits=(), inc=None):
        ws = [w for w in waits if w is not None]
        self.ops.append(dict(eng=eng, fn=fn, waits=ws, inc=inc))
        for w in ws:
            self.awaited.add(w)

    def number(self):
        self.evt = {}
        cnt = {}
        dma_idx = {}
        for op in self.ops:
            eng, inc = op["eng"], op["inc"]
            op["self_wait"] = None
            if eng in DMA_ENGS:
                i = dma_idx.get(eng, 0)
                dma_idx[eng] = i + 1
                sem = f"{eng}{i % NDS}"
                if i >= NDS:
                    op["self_wait"] = (sem, 16 * (i // NDS))
                assert inc is not None
                self.evt[inc] = (sem, 16 * (i // NDS + 1))
                op["do_inc"] = True
                op["sem"] = sem
            elif inc is not None and inc in self.awaited:
                n = cnt.get(eng, 0) + 1
                cnt[eng] = n
                self.evt[inc] = (eng, n)
                op["do_inc"] = True
                op["sem"] = eng
            else:
                op["do_inc"] = False
            op["phys"] = PHYS[eng]
        for op in self.ops:
            for w in op["waits"]:
                assert w in self.evt, f"wait on unregistered event {w}"
        globals()["LAST_EVT"] = dict(self.evt)
        self.dma_final = {}
        for eng in DMA_ENGS:
            n = dma_idx.get(eng, 0)
            for j in range(NDS):
                k = (n - j + NDS - 1) // NDS
                if k > 0:
                    self.dma_final[f"{eng}{j}"] = 16 * k

    def run(self, phys_name, engine, sems, ranges=None, drains=None,
            collect=None):
        maxw = {}

        def wait(sem, val):
            if maxw.get(sem, -1) >= val:
                return
            maxw[sem] = val
            engine.wait_ge(sems[sem], val)

        idx = 0
        for op in self.ops:
            if op["phys"] != phys_name:
                continue
            for w in op["waits"]:
                sem, val = self.evt[w]
                if sem not in DMA_SEM_NAMES and PHYS.get(sem) == phys_name:
                    continue
                wait(sem, val)
            if op["self_wait"] is not None:
                wait(*op["self_wait"])
            if drains is not None and (phys_name, idx) in drains:
                engine.drain()
            instr = op["fn"](engine)
            if op["do_inc"]:
                instr.then_inc(sems[op["sem"]],
                               16 if op["eng"] in DMA_ENGS else 1)
            if collect is not None and instr is not None:
                collect.setdefault(phys_name, []).append(
                    _rw_ranges(instr.ins, ranges))
            idx += 1


def _rw_ranges(mi, ranges):
    def conv(ap_list):
        out = []
        for x in ap_list:
            memref = getattr(x, "memref", None)
            if memref is None:
                continue
            off = x.offset
            span = 1
            for step, count in list(x.ap)[1:]:
                span += abs(step) * (count - 1)
            sz = mybir.dt.size(x.dtype)
            base = ranges.get(memref)
            if base is not None:
                out.append(("SB", base[0] + off * sz, base[0] + (off + span) * sz))
            else:
                out.append((memref, off * sz, (off + span) * sz))
        return out
    return conv(mi.ins), conv(mi.outs)


def _overlap(a, b):
    for sa, la, ha in a:
        for sb, lb, hb in b:
            if sa == sb and la < hb and lb < ha:
                return True
    return False


def _drain_set(collect):
    drains = set()
    for eng, seq in collect.items():
        if eng in ("sync", "tensor"):
            continue
        for i in range(1, len(seq)):
            r, w = seq[i]
            for j in (i - 1, i - 2):
                if j < 0:
                    break
                pr, pw = seq[j]
                if _overlap(w, pw) or _overlap(r, pw) or _overlap(w, pr):
                    drains.add((eng, i))
                    break
                if (eng, j) in drains:
                    break
    return drains


class Mem:
    def __init__(self, nc):
        self.nc = nc
        self.off = SBUF_BASE
        self.peak = SBUF_BASE
        self.ranges = {}

    def alloc(self, name, shape, dtype, at=None):
        size = shape[1] * mybir.dt.size(dtype)
        if at is None:
            at = self.off
            self.off = at + ((size + 63) // 64) * 64
        self.peak = max(self.peak, at + size)
        assert self.peak <= SBUF_TOP, f"SBUF overflow at {name}: {self.peak}"
        h = self.nc.alloc_sbuf_tensor_at(name, list(shape), dtype, offset=at)
        self.ranges[h.name] = (at, at + size)
        return h

    def reserve(self, size):
        at = self.off
        self.off = at + ((size + 63) // 64) * 64
        self.peak = max(self.peak, self.off)
        assert self.peak <= SBUF_TOP
        return at


DEBUG_TAPS = []
TAP_POINTS = set()
TAPS = {}
BUILD_LIMIT = None
MAMBA_STAGE = None
XP_VARIANT = "full"
N_MM1 = 16
MM1_RHS = "xnT"
MM1_EVAC = True
MM1_PS128 = False
NO_TRANSPOSE = False


def _phase_on(name):
    order = ["startup", "mamba", "gateup", "down", "tail"]
    if BUILD_LIMIT is None:
        return True
    return order.index(name) <= order.index(BUILD_LIMIT)


def _mstage_on(name):
    order = ["ln", "xp", "z", "conv", "params", "scan", "y"]
    if MAMBA_STAGE is None:
        return True
    return order.index(name) <= order.index(MAMBA_STAGE)


def build_program():
    collect = {}
    _build(collect=collect, drains=None)
    drains = _drain_set(collect)
    return _build(collect=None, drains=drains)


def _build(collect, drains):
    nc = bass.Bass("TRN2", target_bir_lowering=False,
                   detect_race_conditions=False)

    x_d = nc.dram_tensor("x", [LEXT, D], F32, kind="ExternalInput").ap()
    wg_d = nc.dram_tensor("wgT", [KIN * 128, D], I32, kind="ExternalInput").ap()
    wu_d = nc.dram_tensor("wuT", [KIN * 128, D], I32, kind="ExternalInput").ap()
    wd_d = nc.dram_tensor("wdT", [4 * KIN * 128, 512], I32, kind="ExternalInput").ap()
    wix_d = nc.dram_tensor("wixT", [D, CI], F32, kind="ExternalInput").ap()
    wiz_d = nc.dram_tensor("wizT", [D, CI], F32, kind="ExternalInput").ap()
    wp_d = nc.dram_tensor("wpT", [CI, 1152], F32, kind="ExternalInput").ap()
    wdt_d = nc.dram_tensor("wdtT", [DTR, NH], F32, kind="ExternalInput").ap()
    wo_d = nc.dram_tensor("woT", [CI, D], F32, kind="ExternalInput").ap()
    ebr_d = nc.dram_tensor("ebr", [NH, CI], F32, kind="ExternalInput").ap()
    id_d = nc.dram_tensor("ident", [128, 128], BF16, kind="ExternalInput").ap()
    ppc_d = nc.dram_tensor("ppc", [128, 158], F32, kind="ExternalInput").ap()
    dsc_d = nc.dram_tensor("dscale", [128, D], F32, kind="ExternalInput").ap()
    out_d = nc.dram_tensor("out", [LOWN, D], F32, kind="ExternalOutput").ap()

    B = Builder()

    with ExitStack() as st:
        mem = Mem(nc)
        al = mem.alloc

        xnT = al("xnT", [128, HCN * 1024], BF16)
        gTok = al("gTok", [128, 8 * D], BF16)
        yT = al("yT", [128, CCN * 1024], BF16)
        ident = al("identS", [128, 128], BF16)
        ebr = al("ebrS", [128, CI], F32)
        ppc = al("ppcS", [128, 158], F32)
        aneg = al("anegS", [128, CCN], F32)
        hcarry = al("hcarryS", [128, CCN], F32)
        wp_bf = al("wp_bf", [128, CCN * 1152], BF16)
        wdt_bf = al("wdt_bf", [128, NH], BF16)
        msum = al("msum", [128, 1], F32)
        ssum = al("ssum", [128, 1], F32)
        meanb = al("meanb", [128, 1], F32)
        msq = al("msq", [128, 1], F32)
        varb = al("varb", [128, 1], F32)
        stdb = al("stdb", [128, 1], F32)
        rstd = al("rstd", [128, 1], F32)
        acc1 = al("acc1", [128, 1], F32)
        acc2 = al("acc2", [128, 1], F32)
        negmr = al("negmr", [128, 1], F32)

        g1 = mem.reserve(32768)
        stage32 = al("stage32", [128, 8192], F32, at=g1)
        hT = al("hT", [128, 16 * 1024], BF16, at=g1)

        g2 = mem.reserve(24576)
        xt = [al(f"xt{i}", [128, D], F32, at=g2 + i * 8192) for i in range(2)]
        xn_tok = al("xn_tok", [128, D], BF16, at=g2 + 16384)
        sqh = al("sqh", [128, 1024], F32, at=g2 + 16384 + 4096)
        i32g = [al(f"i32g{i}", [128, 1024], I32, at=g2 + i * 4096) for i in range(3)]
        i32u = [al(f"i32u{i}", [128, 1024], I32, at=g2 + 12288 + i * 4096) for i in range(3)]
        xres = [al(f"xres{i}", [128, 1024], F32, at=g2 + i * 4096) for i in range(2)]
        otile = [al(f"otile{i}", [128, D], F32, at=g2 + 8192 + i * 8192) for i in range(2)]
        wostg = [al(f"wostg{i}", [128, D], F32, at=g2 + i * 8192) for i in range(2)]

        g3 = mem.reserve(16384)
        xnTblk = al("xnTblk", [128, HCN * 512], BF16, at=g3)
        wiz_bf = al("wiz_bf", [128, HCN * 512], BF16, at=g3)
        bfg = [al(f"bfg{i}", [128, 1024], BF16, at=g3 + i * 2048) for i in range(3)]
        bfu = [al(f"bfu{i}", [128, 1024], BF16, at=g3 + 6144 + i * 2048) for i in range(3)]
        wo_bf = al("wo_bf", [128, CCN * D], BF16, at=g3)

        G4SZ = 32768
        g4 = mem.reserve(G4SZ)
        o = g4
        xp = al("xp", [128, CCN * 520], BF16, at=o); o += 4160
        convacc = al("convacc", [128, 512], F32, at=o); o += 2048
        x_act = al("x_act", [128, CCN * 512], BF16, at=o); o += 4096
        dt_blk = al("dt_blk", [128, 512], BF16, at=o); o += 1024
        b_blk = al("b_blk", [128, 512], BF16, at=o); o += 1024
        c_blk = al("c_blk", [128, 512], BF16, at=o); o += 1024
        spe = al("spe", [128, 512], F32, at=o); o += 2048
        delta = al("delta", [128, 512], F32, at=o); o += 2048
        a_blk = al("a_blk", [128, 512], F32, at=o); o += 2048
        bu_blk = al("bu_blk", [128, 512], F32, at=o); o += 2048
        h_blk = al("h_blk", [128, 512], F32, at=o); o += 2048
        t1 = al("t1", [128, 512], F32, at=o); o += 2048
        t1b = al("t1b", [128, 512], F32, at=o); o += 2048
        siluz = al("siluz", [128, CCN * 512], BF16, at=o); o += 4096
        assert o - g4 <= G4SZ, o - g4
        o = g4
        tsilu = [al(f"tsilu{i}", [128, 512], BF16, at=o + i * 1024) for i in range(4)]
        o += 4096
        i32d = [al(f"i32d{i}", [128, 512], I32, at=o + i * 2048) for i in range(2)]
        o += 4096
        bfd = [al(f"bfd{i}", [128, 512], BF16, at=o + i * 1024) for i in range(2)]
        o += 2048
        t5 = al("t5", [128, 512], F32, at=o); o += 2048
        t6 = al("t6", [128, 512], F32, at=o); o += 2048
        t7 = al("t7", [128, 512], F32, at=o); o += 2048
        t5b = al("t5b", [128, 512], F32, at=o); o += 2048
        t6b = al("t6b", [128, 512], F32, at=o); o += 2048
        assert o - g4 <= G4SZ

        wizstg = [al("wizstg0", [128, 512], F32)] * 2
        g5 = mem.reserve(16384)
        wix_bf = al("wix_bf", [128, HCN * 512], BF16, at=g5)
        dscale = al("dscaleS", [128, D], F32, at=g5)

        ps = [st.enter_context(nc.psum_tensor(f"ps{i}", [128, 512], F32))
              for i in range(8)]
        bank_readers = {i: [] for i in range(8)}

        sems = {}
        for name in ("tensor", "vector", "scalar", "gpc"):
            sems[name] = st.enter_context(nc.semaphore(f"sem_{name}"))
        for q in ("sync", "gp"):
            for j in range(NDS):
                sems[f"{q}{j}"] = st.enter_context(nc.semaphore(f"sem_{q}{j}"))

        pe_pending = []

        def pe_mm(fn, ncols, waits=(), inc=None, inc_direct=None):
            for p in pe_pending:
                p[1] -= ncols
            carry = inc_direct
            if carry is None and pe_pending and pe_pending[0][1] <= 0:
                carry = pe_pending.pop(0)[0]
            B.emit("tensor", fn, waits=waits, inc=carry)
            if inc is not None:
                pe_pending.append([inc, 256])

        def pe_flush():
            while pe_pending:
                evt, _ = pe_pending.pop(0)
                B.emit("tensor", (lambda e: e.drain()), inc=evt)

        def mm_group(bank, pairs, out_ap, inc, waits_first=(), waits_at=None,
                     ncols=512):
            n = len(pairs)
            prev = bank_readers[bank]
            bank_readers[bank] = []
            for i, (l, r) in enumerate(pairs):
                w = list(waits_first) + (prev if i == 0 else [])
                if waits_at and i in waits_at:
                    w = w + waits_at[i]
                pe_mm((lambda e, l=l, r=r, s=(i == 0), p=(i == n - 1), oa=out_ap:
                       e.matmul(oa, l, r, start=s, stop=p)),
                      ncols, waits=w, inc=inc if i == n - 1 else None)

        def read_bank(bank, evt):
            bank_readers[bank].append(evt)

        def tap(eng, label, handle):
            if label not in TAP_POINTS:
                return
            import concourse.bass_interp as BI

            def fn(e, h=handle, label=label):
                def snap(core, h=h, label=label):
                    TAPS[label] = np.asarray(core.tensor(h.name)).copy()
                return BI.add_callback(e, snap)
            B.emit(eng, fn)

        B.emit("sync", lambda e: e.dma_start(ident[:], id_d), inc="d_ident")
        B.emit("sync", lambda e: e.dma_start(ebr[0:NH, :], ebr_d), inc="d_ebr")
        B.emit("sync", lambda e: e.dma_start(ppc[:], ppc_d), inc="d_ppc")
        B.emit("scalar", lambda e: e.activation(aneg[:], ppc[:, ALOG:ALOG + 4], AF.Exp),
               waits=["d_ppc"], inc="s_aneg0")
        B.emit("vector", lambda e: e.tensor_scalar_mul(aneg[:], aneg[:], -1.0),
               waits=["s_aneg0"], inc="v_aneg")
        for c in range(HCN):
            B.emit("sync", (lambda e, c=c: e.dma_start(
                stage32[:, c * 512:(c + 1) * 512], wix_d[c * 128:(c + 1) * 128, :])),
                inc=f"d_wix{c}")
        B.emit("vector", lambda e: e.tensor_copy(wix_bf[:], stage32[:]),
               waits=[f"d_wix{c}" for c in range(HCN - NDS, HCN)], inc="v_cwix")
        for c in range(CCN):
            B.emit("sync", (lambda e, c=c: e.dma_start(
                stage32[:, c * 1152:(c + 1) * 1152], wp_d[c * 128:(c + 1) * 128, :])),
                waits=["v_cwix"] if c == 0 else (), inc=f"d_wp{c}")
        B.emit("vector", lambda e: e.tensor_copy(wp_bf[:], stage32[:, 0:CCN * 1152]),
               waits=[f"d_wp{c}" for c in range(CCN)], inc="v_cwp")
        B.emit("sync", lambda e: e.dma_start(stage32[:, 4608:4640], wdt_d),
               waits=["v_cwp"], inc="d_wdt")
        B.emit("vector", lambda e: e.tensor_copy(wdt_bf[:], stage32[:, 4608:4640]),
               waits=["d_wdt"], inc="v_cwdt")

        gtok_at = mem.ranges[gTok.name][0]
        i32e = [al(f"i32e{i}", [128, 1024], I32, at=gtok_at + i * 4096)
                for i in range(4)]
        bfge = [al(f"bfge{i}", [128, 1024], BF16, at=gtok_at + 16384 + i * 2048)
                for i in range(3)]
        bfue = [al(f"bfue{i}", [128, 1024], BF16, at=gtok_at + 22528 + i * 2048)
                for i in range(3)]
        tsie = [al(f"tsie{i}", [128, 512], BF16, at=gtok_at + 28672 + i * 1024)
                for i in range(4)]

        zmm_last = None
        ln_done_evts = []
        blk_trdone = {}
        hst_evts = {}
        ffn_k = [0]
        load_k = [0]
        feed_cap = [99]
        ln_order = [8, 9, 10, 11, 12, 13, 14, 15, 0, 1, 2, 3, 4, 5, 6, 7]
        ln_pos = {g: s for s, g in enumerate(ln_order)}

        def feed_load(k):
            if k >= 16:
                return
            for wi, which in enumerate(("g", "u")):
                src_d = wg_d if which == "g" else wu_d
                bfl = bfge if which == "g" else bfue
                for half in range(2):
                    gi = 2 * k + half
                    islot = wi * 2 + half
                    wt = []
                    if k >= 1:
                        wt.append(f"s_c{which}{2*(k-1)+half}")
                    B.emit("sync", (lambda e, src_d=src_d, dst=i32e[islot], k=k,
                                    lo=half * 1024: e.dma_start(
                        dst[:], src_d[k * 128:(k + 1) * 128, lo:lo + 1024])),
                        waits=wt, inc=f"d_{which}{gi}")
                    cwt = [f"d_{which}{gi}"]
                    if gi >= 3:
                        cwt.append(f"t_gu{(gi-3)//2}_{which}1")
                    B.emit("scalar", (lambda e, bfs=bfl[gi % 3],
                                      i32s=i32e[islot]: e.activation(
                        bfs[:], i32s[:], AF.Copy, bias=-128.0)),
                        waits=cwt, inc=f"s_c{which}{gi}")

        def ffn_feed(nk):
            if not _phase_on("gateup"):
                return
            for _ in range(nk):
                if feed_cap[0] <= 0:
                    return
                feed_cap[0] -= 1
                k = ffn_k[0]
                if k >= 16:
                    return
                ffn_k[0] = k + 1
                pe_flush()
                while load_k[0] <= min(k + 1, 15):
                    feed_load(load_k[0])
                    load_k[0] += 1
                for which in ("g", "u"):
                    wslots = bfge if which == "g" else bfue
                    for tch in range(2):
                        bank = 4 + (0 if which == "g" else 2) + tch
                        pairs = []
                        for c in range(HCN):
                            half = c // 8
                            sl = wslots[(2 * k + half) % 3]
                            pairs.append((sl[:, (c % 8) * 128:(c % 8 + 1) * 128],
                                          xnT[:, c * 1024 + tch * 512: c * 1024 + tch * 512 + 512]))
                        wa = {0: [f"s_c{which}{2*k}"], 8: [f"s_c{which}{2*k+1}"]}
                        if k == 0:
                            wa[0] = wa[0] + blk_trdone[2] + blk_trdone[3]
                        mm_group(bank, pairs, ps[bank][:],
                                 inc=f"t_gu{k}_{which}{tch}", waits_at=wa)
                for tch in range(2):
                    gbank = 4 + tch
                    ubank = 6 + tch
                    tslot = (k % 2) * 2 + tch
                    sw = [f"t_gu{k}_g{tch}"]
                    if k >= 2:
                        sw.append(f"v_h{k-2}_{tch}")
                    B.emit("scalar", (lambda e, k=k, tslot=tslot, gbank=gbank:
                           e.activation(
                        tsie[tslot][:], ps[gbank][:], AF.Silu,
                        scale=ppc[:, GSC + k:GSC + k + 1])),
                        waits=sw, inc=f"s_sil{k}_{tch}")
                    read_bank(gbank, f"s_sil{k}_{tch}")
                    hw = [f"t_gu{k}_u{tch}", f"s_sil{k}_{tch}"]
                    if k == 0:
                        hw.append("v_cwdt")
                    B.emit("vector", (lambda e, k=k, tch=tch, tslot=tslot,
                                      ubank=ubank: e.scalar_tensor_tensor(
                        hT[:, (k % 16) * 1024 + tch * 512:(k % 16) * 1024 + tch * 512 + 512],
                        ps[ubank][:], ppc[:, USC + k:USC + k + 1],
                        tsie[tslot][:], ALU.mult, ALU.mult)),
                        waits=hw, inc=f"v_h{k}_{tch}")
                    read_bank(ubank, f"v_h{k}_{tch}")
                    hst_evts[(k, tch)] = f"v_h{k}_{tch}"
                pe_flush()

        def emit_ln_tile(g, fw):
            s = ln_pos[g]
            b, j = g // 4, g % 4
            p1 = ln_order[s - 1] if s >= 1 else None
            p2 = ln_order[s - 2] if s >= 2 else None
            slot = s % 2
            if b >= 2:
                def dst(c, b=b, j=j):
                    lo = c * 1024 + (b - 2) * 512 + j * 128
                    return xnT[:, lo:lo + 128]
            else:
                def dst(c, j=j):
                    return xnTblk[:, c * 512 + j * 128: c * 512 + j * 128 + 128]
            wt = []
            if p2 is not None:
                wt = [f"s_norm{p2}", f"v_red{p2}"]
            B.emit("gp", (lambda e, g=g, slot=slot: e.dma_start(
                xt[slot][:], x_d[g * 128:(g + 1) * 128, :])),
                waits=wt, inc=f"d_x{g}")
            sq_w = [f"d_x{g}"] + fw("scalar")
            if p1 is not None:
                sq_w += [f"v_red{p1}"]
            B.emit("scalar", (lambda e, slot=slot: e.activation(
                sqh[:], xt[slot][:, 0:1024], AF.Square, accum_out=acc1[:])),
                waits=sq_w, inc=None)
            B.emit("scalar", (lambda e, slot=slot: e.activation(
                sqh[:], xt[slot][:, 1024:2048], AF.Square, accum_out=acc2[:])),
                inc=f"s_sq{g}")
            B.emit("vector", (lambda e, slot=slot: e.reduce_sum(
                msum[:], xt[slot][:], axis=mybir.AxisListType.X)),
                waits=[f"d_x{g}"] + fw("vector"))
            B.emit("vector", (lambda e: e.tensor_add(
                ssum[:], acc1[:], acc2[:])),
                waits=[f"s_sq{g}"], inc=f"v_red{g}")
            B.emit("vector", lambda e: e.tensor_scalar_mul(meanb[:], msum[:], 1.0 / D),
                   waits=[f"s_norm{p1}"] if p1 is not None else [])
            B.emit("vector", lambda e: e.tensor_scalar(
                msq[:], meanb[:], meanb[:], 1e-5, ALU.mult, ALU.subtract))
            B.emit("vector", lambda e: e.tensor_scalar(
                varb[:], ssum[:], 1.0 / D, msq[:], ALU.mult, ALU.subtract))
            B.emit("vector", lambda e: e.tensor_scalar_mul(negmr[:], meanb[:], -1.0),
                   inc=f"v_lnt{g}")
            B.emit("scalar", lambda e: e.activation(stdb[:], varb[:], AF.Sqrt),
                   waits=[f"v_lnt{g}"], inc=f"s_std{g}")
            B.emit("vector", lambda e: e.reciprocal(rstd[:], stdb[:]),
                   waits=[f"s_std{g}"])
            B.emit("vector", (lambda e, slot=slot: e.tensor_scalar(
                xn_tok[:], xt[slot][:], negmr[:], rstd[:], ALU.add, ALU.mult)),
                waits=[f"t_tr{p1}_{HCN-1}"] if p1 is not None else [],
                inc=f"s_norm{g}")
            for c in range(HCN):
                bank = c % 4
                prev = bank_readers[bank]
                bank_readers[bank] = []
                pe_mm((lambda e, c=c, bank=bank: e.matmul(
                    ps[bank][:, 0:128], xn_tok[:, c * 128:(c + 1) * 128],
                    ident[:], start=True, stop=True)), 128,
                    waits=[f"s_norm{g}", "d_ident"] + prev + fw("tensor"),
                    inc=f"t_tr{g}_{c}")
                B.emit("vector", (lambda e, c=c, bank=bank, dst=dst:
                       e.tensor_copy(dst(c), ps[bank][:, 0:128])),
                       waits=[f"t_tr{g}_{c}"], inc=f"v_tr{g}_{c}")
                read_bank(bank, f"v_tr{g}_{c}")
            pe_flush()

        for si, g in enumerate(ln_order[:8]):
            emit_ln_tile(g, lambda eng: [])
            if si == 5 and _phase_on("gateup"):
                feed_load(0)
                load_k[0] = 1
        for b in (2, 3):
            blk_trdone[b] = [f"v_tr{b*4+j}_{c}" for j in range(4)
                             for c in range(HCN)]
        ffn_feed(3)

        for b in range(NBLK):
            own = b >= 2
            if own:
                def xcols(c, b=b):
                    lo = c * 1024 + (b - 2) * 512
                    return xnT[:, lo:lo + 512]
            else:
                def xcols(c):
                    return xnTblk[:, c * 512:(c + 1) * 512]

            if b >= 1:
                first_w = {"scalar": [f"bar_v{b-1}", f"bar_t{b-1}"],
                           "vector": [f"bar_s{b-1}", f"bar_t{b-1}"],
                           "tensor": [f"bar_s{b-1}", f"bar_v{b-1}"]}
            else:
                first_w = {"scalar": [], "vector": [], "tensor": []}
            fw_used = {k: False for k in first_w}

            def fw(eng):
                if fw_used[eng]:
                    return []
                fw_used[eng] = True
                return first_w[eng]

            feed_cap[0] = {0: 3, 1: 4, 2: 5, 3: 8}[b]
            if not own:
                for j in range(4):
                    emit_ln_tile(b * 4 + j, fw)
                blk_trdone[b] = [f"v_tr{b*4+j}_{c}" for j in range(4)
                                 for c in range(HCN)]
                if b == 1:
                    g = 7
                    ln_done_evts += [f"s_norm{g}", f"v_red{g}",
                                     f"t_tr{g}_{HCN-1}"]
            trdone = blk_trdone[b]
            ffn_feed(1)

            for cc in range(CCN if (_mstage_on("xp") and XP_VARIANT == "full") else 0):
                last = (cc == CCN - 1 and b != 2)
                if b == 0:
                    B.emit("vector", (lambda e, cc=cc: e.memset(
                        xp[:, cc * 520:cc * 520 + 3], 0.0)),
                        inc="v_carry0" if last else None)
                else:
                    B.emit("vector", (lambda e, cc=cc: e.tensor_copy(
                        xp[:, cc * 520:cc * 520 + 3],
                        xp[:, cc * 520 + 512:cc * 520 + 515])),
                        waits=fw("vector"),
                        inc=f"v_carry{b}" if last else None)
                    if b == 2:
                        B.emit("vector", (lambda e, cc=cc: e.tensor_scalar_mul(
                            xp[:, cc * 520:cc * 520 + 3],
                            xp[:, cc * 520:cc * 520 + 3],
                            ppc[:, PMASK:PMASK + 1])),
                            inc=f"v_carry{b}" if cc == CCN - 1 else None)

            for cc in range(CCN if _mstage_on("xp") else 0):
                bank = 0 + (cc % 2)
                pairs = [(wix_bf[:, c * 512 + cc * 128: c * 512 + (cc + 1) * 128],
                          xcols(c)) for c in range(HCN)]
                mm_group(bank, pairs, ps[bank][:], inc=f"t_xp{b}_{cc}",
                         waits_first=(["v_cwix"] + trdone + fw("tensor"))
                         if cc == 0 else [])
                B.emit("vector", (lambda e, cc=cc, bank=bank: e.tensor_copy(
                    xp[:, cc * 520 + 3:cc * 520 + 515], ps[bank][:])),
                    waits=[f"t_xp{b}_{cc}", f"v_carry{b}"], inc=f"v_xp{b}_{cc}")
                read_bank(bank, f"v_xp{b}_{cc}")
            if _mstage_on("xp"):
                pe_flush()
            if b == 1 and _mstage_on("z"):
                for c in range(HCN):
                    wwt = [f"t_xp1_{CCN-1}"] if c < 1 else [f"v_wz{c-1}"]
                    B.emit("sync", (lambda e, c=c: e.dma_start(
                        wizstg[c % 2][:], wiz_d[c * 128:(c + 1) * 128, :])),
                        waits=wwt, inc=f"d_wiz{c}")
                    B.emit("vector", (lambda e, c=c: e.tensor_copy(
                        wiz_bf[:, c * 512:(c + 1) * 512], wizstg[c % 2][:])),
                        waits=[f"d_wiz{c}"], inc=f"v_wz{c}")
                B.emit("vector", lambda e: e.nop(), inc="v_cwiz")
            ffn_feed(1)

            if own and _mstage_on("z"):
                for cc in range(CCN):
                    bank = 2 + (cc % 2)
                    pairs = [(wiz_bf[:, c * 512 + cc * 128: c * 512 + (cc + 1) * 128],
                              xcols(c)) for c in range(HCN)]
                    mm_group(bank, pairs, ps[bank][:], inc=f"t_z{b}_{cc}",
                             waits_first=["v_cwiz"] if (b == 2 and cc == 0) else [])
                    B.emit("scalar", (lambda e, cc=cc, bank=bank: e.activation(
                        siluz[:, cc * 512:(cc + 1) * 512], ps[bank][:], AF.Silu)),
                        waits=[f"t_z{b}_{cc}"] + fw("scalar"), inc=f"s_z{b}_{cc}")
                    read_bank(bank, f"s_z{b}_{cc}")
                    zmm_last = f"t_z{b}_{cc}"
                pe_flush()

            for cc in range(CCN if _mstage_on("conv") else 0):
                base = cc * 520
                B.emit("vector", (lambda e, base=base, cc=cc: e.tensor_scalar_mul(
                    convacc[:], xp[:, base:base + 512],
                    ppc[:, CONVW + cc * 4:CONVW + cc * 4 + 1])),
                    waits=[f"v_xp{b}_{cc}"]
                    + ([f"s_xact{b}_{cc-1}"] if cc >= 1 else []))
                for k in range(1, 4):
                    B.emit("vector", (lambda e, base=base, cc=cc, k=k:
                           e.scalar_tensor_tensor(
                               convacc[:], xp[:, base + k:base + 512 + k],
                               ppc[:, CONVW + cc * 4 + k:CONVW + cc * 4 + k + 1],
                               convacc[:], ALU.mult, ALU.add)),
                           inc=f"v_conv{b}_{cc}" if k == 3 else None)
                B.emit("scalar", (lambda e, cc=cc: e.activation(
                    x_act[:, cc * 512:(cc + 1) * 512], convacc[:], AF.Silu,
                    bias=ppc[:, CONVB + cc:CONVB + cc + 1])),
                    waits=[f"v_conv{b}_{cc}"], inc=f"s_xact{b}_{cc}")
                tap("scalar", f"xact{b}_{cc}", x_act)
                if cc == 1 or cc == 3:
                    ffn_feed(1)

            xact_w = [f"s_xact{b}_{cc}" for cc in range(CCN)]

            if not _mstage_on("params"):
                pe_flush()
                B.emit("tensor", lambda e: e.nop(), inc=f"bar_t{b}")
                B.emit("vector", lambda e: e.nop(), inc=f"bar_v{b}")
                B.emit("scalar", lambda e: e.nop(), inc=f"bar_s{b}")
                continue
            bank = 0
            pairs = [(wp_bf[:, cc * 1152: cc * 1152 + 128],
                      x_act[:, cc * 512:(cc + 1) * 512]) for cc in range(CCN)]
            mm_group(bank, pairs, ps[bank][:], inc=f"t_dt{b}",
                     waits_first=xact_w + ["v_cwp"])
            pe_flush()
            B.emit("vector", (lambda e: e.tensor_copy(dt_blk[:], ps[0][:])),
                   waits=[f"t_dt{b}"], inc=f"v_dt{b}")
            tap("vector", f"dt{b}", dt_blk)
            read_bank(0, f"v_dt{b}")
            mm_group(1, [(wdt_bf[:], dt_blk[:])], ps[1][0:NH, :],
                     inc=f"t_del{b}", waits_first=[f"v_dt{b}", "v_cwdt"])
            pe_flush()
            B.emit("scalar", (lambda e: e.activation(
                spe[0:NH, :], ps[1][0:NH, :], AF.Exp, bias=ppc[0:NH, BDT:BDT + 1])),
                waits=[f"t_del{b}"], inc=f"s_spe{b}")
            read_bank(1, f"s_spe{b}")
            B.emit("scalar", lambda e: e.activation(
                delta[0:NH, :], spe[0:NH, :], AF.Ln, bias=1.0), inc=f"s_del{b}")
            tap("scalar", f"delta{b}", delta)
            ffn_feed(1)

            for q in range(CCN if _mstage_on("scan") else 0):
                bank = 0 + (q % 2)
                pairs = [(wp_bf[:, cc * 1152 + (1 + q) * 128: cc * 1152 + (2 + q) * 128],
                          x_act[:, cc * 512:(cc + 1) * 512]) for cc in range(CCN)]
                mm_group(bank, pairs, ps[bank][:], inc=f"t_pb{b}_{q}")
                B.emit("vector", (lambda e, bank=bank: e.tensor_copy(
                    b_blk[:], ps[bank][:])),
                    waits=[f"t_pb{b}_{q}"], inc=f"v_pb{b}_{q}")
                tap("vector", f"b{b}_{q}", b_blk)
                read_bank(bank, f"v_pb{b}_{q}")
                if own:
                    bank = 2 + (q % 2)
                    pairs = [(wp_bf[:, cc * 1152 + (5 + q) * 128: cc * 1152 + (6 + q) * 128],
                              x_act[:, cc * 512:(cc + 1) * 512]) for cc in range(CCN)]
                    mm_group(bank, pairs, ps[bank][:], inc=f"t_pc{b}_{q}")
                    B.emit("vector", (lambda e, bank=bank: e.tensor_copy(
                        c_blk[:], ps[bank][:])),
                        waits=[f"t_pc{b}_{q}"], inc=f"v_pc{b}_{q}")
                    read_bank(bank, f"v_pc{b}_{q}")
                bcb = (0 if own else 2) + (q % 2)
                mm_group(bcb, [(ebr[0:NH, q * 128:(q + 1) * 128], delta[0:NH, :])],
                         ps[bcb][:], inc=f"t_bc{b}_{q}",
                         waits_first=[f"s_del{b}", "d_ebr"])
                scal = 1.0 if own else ppc[:, PMASK:PMASK + 1]
                B.emit("vector", (lambda e, bcb=bcb, scal=scal: e.scalar_tensor_tensor(
                    t1[:], ps[bcb][:], scal, b_blk[:], ALU.mult, ALU.mult)),
                    waits=[f"t_bc{b}_{q}", f"v_pb{b}_{q}"], inc=f"v_t1{b}_{q}")
                B.emit("vector", (lambda e, q=q: e.tensor_mul(
                    bu_blk[:], t1[:], x_act[:, q * 512:(q + 1) * 512])),
                    inc=f"v_bu{b}_{q}")
                tap("vector", f"bu{b}_{q}", bu_blk)
                tap("vector", f"t1_{b}_{q}", t1)
                aw = [f"v_t1{b}_{q}", "v_aneg"]
                if q > 0:
                    aw.append(f"v_scan{b}_{q-1}")
                elif b > 0:
                    aw.append(f"bar_v{b-1}")
                B.emit("scalar", (lambda e, q=q, bcb=bcb: e.activation(
                    a_blk[:], ps[bcb][:], AF.Exp, scale=aneg[:, q:q + 1])),
                    waits=aw, inc=f"s_a{b}_{q}")
                read_bank(bcb, f"s_a{b}_{q}")
                init = 0.0 if b == 0 else hcarry[:, q:q + 1]
                B.emit("vector", (lambda e, init=init: e.tensor_tensor_scan(
                    h_blk[:], a_blk[:], bu_blk[:], init, ALU.mult, ALU.add)),
                    waits=[f"s_a{b}_{q}"], inc=f"v_scan{b}_{q}")
                B.emit("vector", (lambda e, q=q: e.tensor_copy(
                    hcarry[:, q:q + 1], h_blk[:, 511:512])))
                ffn_feed(1)
                if own:
                    B.emit("vector", (lambda e: e.tensor_mul(
                        t1[:], h_blk[:], c_blk[:])),
                        waits=[f"v_pc{b}_{q}"])
                    B.emit("vector", (lambda e, q=q: e.scalar_tensor_tensor(
                        t1b[:], x_act[:, q * 512:(q + 1) * 512],
                        ppc[:, DVEC + q:DVEC + q + 1], t1[:], ALU.mult, ALU.add)))
                    B.emit("vector", (lambda e, q=q, b=b: e.tensor_mul(
                        yT[:, q * 1024 + (b - 2) * 512: q * 1024 + (b - 2) * 512 + 512],
                        t1b[:], siluz[:, q * 512:(q + 1) * 512])),
                        waits=[f"s_z{b}_{q}"], inc=f"v_y{b}_{q}")

            pe_flush()
            B.emit("tensor", lambda e: e.nop(), inc=f"bar_t{b}")
            B.emit("vector", lambda e: e.nop(), inc=f"bar_v{b}")
            B.emit("scalar", lambda e: e.nop(), inc=f"bar_s{b}")

        mamba_done = [f"bar_t{NBLK-1}", f"bar_v{NBLK-1}", f"bar_s{NBLK-1}"]
        feed_cap[0] = 99
        ffn_feed(16)

        B.emit("sync", lambda e: e.dma_start(dscale[:], dsc_d),
               waits=[f"t_xp{NBLK-1}_{CCN-1}"]
               if (_mstage_on("xp") and XP_VARIANT not in ("mm1",)) else [],
               inc="d_dsc")

        down_done = []
        for q4 in range(NQ if _phase_on("gateup") else 0):
            for kk in range(16 if q4 >= 1 else 0):
                k = q4 * 16 + kk
                for which in ("g", "u"):
                    src = wg_d if which == "g" else wu_d
                    i32s_l = i32g if which == "g" else i32u
                    bfs_l = bfg if which == "g" else bfu
                    for half in range(2):
                        gi = 2 * k + half
                        slot = gi % 3
                        wt = []
                        if k == 16 and half == 0 and which == "g":
                            wt = list(ln_done_evts)
                        if gi >= 3:
                            wt.append(f"s_c{which}{gi-3}")
                        B.emit("sync", (lambda e, src=src, dst=i32s_l[slot], k=k,
                                        lo=half * 1024: e.dma_start(
                            dst[:], src[k * 128:(k + 1) * 128, lo:lo + 1024])),
                            waits=wt, inc=f"d_{which}{gi}")
                        cwt = [f"d_{which}{gi}"]
                        if k == 16 and half == 0:
                            cwt.append(zmm_last)
                        if gi >= 3:
                            cwt.append(f"t_gu{(gi-3)//2}_{which}1")
                        B.emit("scalar", (lambda e, bfs=bfs_l[slot], i32s=i32s_l[slot]:
                               e.activation(bfs[:], i32s[:], AF.Copy, bias=-128.0)),
                            waits=cwt, inc=f"s_c{which}{gi}")
                for which in ("g", "u"):
                    wslots = bfg if which == "g" else bfu
                    for tch in range(2):
                        bank = (k % 2) * 4 + (0 if which == "g" else 2) + tch
                        pairs = []
                        for c in range(HCN):
                            half = c // 8
                            sl = wslots[(2 * k + half) % 3]
                            pairs.append((sl[:, (c % 8) * 128:(c % 8 + 1) * 128],
                                          xnT[:, c * 1024 + tch * 512: c * 1024 + tch * 512 + 512]))
                        wa = {0: [f"s_c{which}{2*k}"] + (mamba_done if k == 16 else []),
                              8: [f"s_c{which}{2*k+1}"]}
                        mm_group(bank, pairs, ps[bank][:],
                                 inc=f"t_gu{k}_{which}{tch}", waits_at=wa)
                for tch in range(2):
                    gbank = (k % 2) * 4 + tch
                    ubank = (k % 2) * 4 + 2 + tch
                    tslot = (k % 2) * 2 + tch
                    sw = [f"t_gu{k}_g{tch}"]
                    if q4 > 0 and kk == 0 and down_done:
                        sw.append(down_done[-1])
                    if k >= 2:
                        sw.append(f"v_h{k-2}_{tch}")
                    B.emit("scalar", (lambda e, k=k, tslot=tslot, gbank=gbank:
                           e.activation(
                        tsilu[tslot][:], ps[gbank][:], AF.Silu,
                        scale=ppc[:, GSC + k:GSC + k + 1])),
                        waits=sw, inc=f"s_sil{k}_{tch}")
                    read_bank(gbank, f"s_sil{k}_{tch}")
                    hw = [f"t_gu{k}_u{tch}", f"s_sil{k}_{tch}"]
                    if q4 > 0 and down_done:
                        hw.append(down_done[-1])
                    B.emit("vector", (lambda e, k=k, tch=tch, tslot=tslot, ubank=ubank:
                           e.scalar_tensor_tensor(
                               hT[:, (k % 16) * 1024 + tch * 512:(k % 16) * 1024 + tch * 512 + 512],
                               ps[ubank][:], ppc[:, USC + k:USC + k + 1],
                               tsilu[tslot][:], ALU.mult, ALU.mult)),
                           waits=hw, inc=f"v_h{k}_{tch}")
                    read_bank(ubank, f"v_h{k}_{tch}")
                    hst_evts[(k, tch)] = f"v_h{k}_{tch}"

            pe_flush()
            if not _phase_on("down"):
                continue
            hT_ready = [hst_evts[(q4 * 16 + kk, tch)] for kk in range(16) for tch in range(2)]
            for hs in range(4):
                for kk in range(16):
                    gidx = (q4 * 4 + hs) * 16 + kk
                    row = (hs * KIN + q4 * 16 + kk) * 128
                    wt = []
                    if gidx < 2:
                        wt = list(mamba_done)
                    else:
                        wt.append(f"s_cd{gidx-2}")
                    B.emit("sync", (lambda e, row=row, sl=gidx % 2: e.dma_start(
                        i32d[sl][:], wd_d[row:row + 128, :])),
                        waits=wt, inc=f"d_wd{gidx}")
                    cwt = [f"d_wd{gidx}"]
                    if gidx < 2:
                        cwt += mamba_done
                    else:
                        p = gidx - 2
                        pq, prem = divmod(p, 64)
                        phs, pkk = divmod(prem, 16)
                        cwt.append(f"t_dn{pq}_{phs}_{pkk}")
                    B.emit("scalar", (lambda e, sl=gidx % 2: e.activation(
                        bfd[sl][:], i32d[sl][:], AF.Copy, bias=-128.0)),
                        waits=cwt, inc=f"s_cd{gidx}")
                    for tc in range(8):
                        prev = bank_readers[tc] if kk == 0 else []
                        if kk == 0:
                            bank_readers[tc] = []
                        w = ([f"s_cd{gidx}"] if tc == 0 else []) + prev
                        if kk == 0 and tc == 0 and hs == 0:
                            w = w + hT_ready
                        pe_mm((lambda e, kk=kk, tc=tc, sl=gidx % 2:
                               e.matmul(ps[tc][:],
                                        hT[:, kk * 1024 + tc * 128: kk * 1024 + tc * 128 + 128],
                                        bfd[sl][:],
                                        start=(kk == 0), stop=(kk == 15))), 512,
                              waits=w,
                              inc_direct=(f"t_dn{q4}_{hs}_{kk}"
                                          if (tc == 7 and kk < 15) else None),
                              inc=(f"t_dn{q4}_{hs}_{kk}"
                                   if (tc == 7 and kk == 15) else None))
                pe_flush()
                for tc in range(8):
                    col = tc * D + hs * 512
                    if q4 == 0:
                        gw = [f"t_dn{q4}_{hs}_15"]
                        if hs == 0 and tc == 0:
                            gw += [f"s_cg{31}", f"s_cu{31}", f"v_h{15}_{1}"]
                        B.emit("vector", (lambda e, col=col, tc=tc: e.tensor_copy(
                            gTok[:, col:col + 512], ps[tc][:])),
                            waits=gw, inc=f"v_dn{q4}_{hs}_{tc}")
                    elif q4 < 3:
                        B.emit("vector", (lambda e, col=col, tc=tc: e.tensor_add(
                            gTok[:, col:col + 512], ps[tc][:], gTok[:, col:col + 512])),
                            waits=[f"t_dn{q4}_{hs}_15"], inc=f"v_dn{q4}_{hs}_{tc}")
                    else:
                        B.emit("vector", (lambda e, col=col, tc=tc: e.tensor_add(
                            gTok[:, col:col + 512], ps[tc][:],
                            gTok[:, col:col + 512])),
                            waits=[f"t_dn{q4}_{hs}_15"], inc=f"v_dn{q4}_{hs}_{tc}")
                    read_bank(tc, f"v_dn{q4}_{hs}_{tc}")
            down_done.append(f"t_dn{q4}_3_15")

        for hs in range(4):
            for tc in range(8):
                col = tc * D + hs * 512
                n8 = hs * 8 + tc
                tb6 = t6 if n8 % 2 == 0 else t6b
                pv = n8 - 2
                gw = [f"v_dn{3}_{hs}_{tc}", "d_dsc"]
                if pv >= 0:
                    gw.append(f"s_g{pv // 8}_{pv % 8}")
                B.emit("vector", (lambda e, col=col, hs=hs, tb6=tb6: e.tensor_mul(
                    tb6[:], gTok[:, col:col + 512],
                    dscale[:, hs * 512:(hs + 1) * 512])),
                    waits=gw, inc=f"v_g{hs}_{tc}")
                B.emit("scalar", (lambda e, col=col, tb6=tb6: e.activation(
                    gTok[:, col:col + 512], tb6[:], AF.Silu)),
                    waits=[f"v_g{hs}_{tc}"], inc=f"s_g{hs}_{tc}")

        if not _phase_on("tail"):
            for tc in range(8):
                B.emit("gp", (lambda e, tc=tc: e.dma_start(
                    out_d[tc * 128:(tc + 1) * 128, :], otile[tc % 2][:])),
                    inc=f"d_out{tc}")
        last_gu_cast = [f"s_cg{2*KIN-1}", f"s_cu{2*KIN-1}"]
        for c in range(CCN if _phase_on("tail") else 0):
            B.emit("sync", (lambda e, c=c: e.dma_start(
                wostg[c % 2][:], wo_d[c * 128:(c + 1) * 128, :])),
                waits=last_gu_cast + ([f"v_cwo{c-2}"] if c >= 2 else []),
                inc=f"d_wo{c}")
            B.emit("vector", (lambda e, c=c: e.tensor_copy(
                wo_bf[:, c * D:(c + 1) * D], wostg[c % 2][:])),
                waits=[f"d_wo{c}", f"t_gu{KIN-1}_u1"],
                inc=f"v_cwo{c}")
        wo_ready = [f"v_cwo{c}" for c in range(CCN)]

        g_ready = [f"s_g{hs}_{tc}" for hs in range(4) for tc in range(8)]
        for tc in range(8 if _phase_on("tail") else 0):
            wt = list(wo_ready)
            if tc >= 2:
                wt.append(f"v_o{tc-2}_3")
            B.emit("gp", (lambda e, tc=tc: e.dma_start(
                xres[tc % 2][:], x_d[1024 + tc * 128: 1024 + tc * 128 + 128, 0:1024])),
                waits=wt, inc=f"d_xr{tc}a")
            for hs in range(4):
                bank = hs % 2
                pairs = [(yT[:, q * 1024 + tc * 128: q * 1024 + tc * 128 + 128],
                          wo_bf[:, q * D + hs * 512: q * D + hs * 512 + 512])
                         for q in range(CCN)]
                wf = wo_ready + [f"s_g{hs}_{tc}"]
                mm_group(bank, pairs, ps[bank][:], inc=f"t_s{tc}_{hs}", waits_first=wf)
                n4 = tc * 4 + hs
                pv4 = n4 - 2
                sow = [f"t_s{tc}_{hs}"]
                if pv4 >= 0:
                    sow.append(f"v_o{pv4 // 4}_{pv4 % 4}")
                B.emit("scalar", (lambda e, bank=bank, tb=(t5 if hs % 2 == 0 else t5b):
                       e.activation(tb[:], ps[bank][:], AF.Copy)),
                    waits=sow, inc=f"s_o{tc}_{hs}")
                B.emit("vector", (lambda e, tc=tc, hs=hs,
                                  tb=(t5 if hs % 2 == 0 else t5b): e.tensor_mul(
                    t7[:], tb[:],
                    gTok[:, tc * D + hs * 512: tc * D + hs * 512 + 512])),
                    waits=[f"s_o{tc}_{hs}"])
                if hs == 2:
                    B.emit("gp", (lambda e, tc=tc: e.dma_start(
                        xres[tc % 2][:],
                        x_d[1024 + tc * 128: 1024 + tc * 128 + 128, 1024:2048])),
                        waits=[f"v_o{tc}_1"], inc=f"d_xr{tc}b")
                xw = f"d_xr{tc}a" if hs < 2 else f"d_xr{tc}b"
                ow = [xw]
                if tc >= 2:
                    ow.append(f"d_out{tc-2}")
                B.emit("vector", (lambda e, tc=tc, hs=hs: e.tensor_add(
                    otile[tc % 2][:, hs * 512:(hs + 1) * 512], t7[:],
                    xres[tc % 2][:, (hs % 2) * 512:(hs % 2) * 512 + 512])),
                    waits=ow, inc=f"v_o{tc}_{hs}")
                read_bank(bank, f"v_o{tc}_{hs}")
            B.emit("gp", (lambda e, tc=tc: e.dma_start(
                out_d[tc * 128:(tc + 1) * 128, :], otile[tc % 2][:])),
                waits=[f"v_o{tc}_{hs}" for hs in range(4)], inc=f"d_out{tc}")
        pe_flush()

        local_bufs = dict(locals())
        for nm, bufname, cols, wevt in DEBUG_TAPS:
            buf = local_bufs[bufname]
            dt_ = buf[:].dtype
            tap_d = nc.dram_tensor(f"dbg_{nm}", [128, cols], dt_,
                                   kind="ExternalOutput").ap()
            B.emit("gp", (lambda e, tap_d=tap_d, buf=buf, cols=cols:
                          e.dma_start(tap_d, buf[:, 0:cols])),
                   waits=[wevt], inc=f"d_tap_{nm}")

        B.number()

        rng = mem.ranges
        with nc.Block() as block:
            @block.sync
            def _(sync):
                B.run("sync", sync, sems, rng, drains, collect)
                for j in range(NDS):
                    if f"sync{j}" in B.dma_final:
                        sync.wait_ge(sems[f"sync{j}"], B.dma_final[f"sync{j}"])

            @block.gpsimd
            def _(gp):
                B.run("gpsimd", gp, sems, rng, drains, collect)
                for j in range(NDS):
                    if f"gp{j}" in B.dma_final:
                        gp.wait_ge(sems[f"gp{j}"], B.dma_final[f"gp{j}"])

            @block.tensor
            def _(tensor):
                B.run("tensor", tensor, sems, rng, drains, collect)

            @block.vector
            def _(vector):
                B.run("vector", vector, sems, rng, drains, collect)

            @block.scalar
            def _(scalar):
                B.run("scalar", scalar, sems, rng, drains, collect)

    return nc


_CACHE = {}


def _host_prep(inputs):
    x = np.ascontiguousarray(np.asarray(inputs["L0_attn_hidden_states"]))

    def gu_tiles(w):
        t = np.asarray(w).reshape(KIN, 128, HCN, 128).transpose(0, 3, 2, 1)
        return np.ascontiguousarray(t.reshape(KIN * 128, D).astype(np.int32))

    wgT = gu_tiles(inputs["L0_ffn_gate_w"])
    wuT = gu_tiles(inputs["L0_ffn_up_w"])
    t = np.asarray(inputs["L0_ffn_down_w"]).reshape(4, 512, KIN, 128).transpose(0, 2, 3, 1)
    wdT = np.ascontiguousarray(t.reshape(4 * KIN * 128, 512).astype(np.int32))

    wixT = np.ascontiguousarray(np.asarray(inputs["L0_attn_W_in_x"]).T.astype(np.float32))
    wizT = np.ascontiguousarray(np.asarray(inputs["L0_attn_W_in_z"]).T.astype(np.float32))
    wpT = np.ascontiguousarray(np.asarray(inputs["L0_attn_W_param"]).T.astype(np.float32))
    wdtT = np.ascontiguousarray(np.asarray(inputs["L0_attn_W_dt"]).T.astype(np.float32))
    woT = np.ascontiguousarray(np.asarray(inputs["L0_attn_W_out"]).T.astype(np.float32))

    ebr = np.zeros((NH, CI), np.float32)
    ebr[np.arange(CI) // 16, np.arange(CI)] = 1.0
    ident = np.eye(128, dtype=ml_dtypes.bfloat16)

    ppc = np.zeros((128, 158), np.float32)
    ppc[:, ALOG:ALOG + 4] = np.asarray(inputs["L0_attn_A_log"]).astype(np.float32)\
        .reshape(CI).reshape(CCN, 128).T
    ppc[:, CONVW:CONVW + 16] = np.asarray(inputs["L0_attn_conv_w"]).astype(np.float32)\
        .reshape(CI, 4).reshape(CCN, 128, 4).transpose(1, 0, 2).reshape(128, 16)
    ppc[:, CONVB:CONVB + 4] = np.asarray(inputs["L0_attn_conv_b"]).astype(np.float32)\
        .reshape(CCN, 128).T
    ppc[:, DVEC:DVEC + 4] = np.asarray(inputs["L0_attn_D"]).astype(np.float32)\
        .reshape(CCN, 128).T
    ppc[0:NH, BDT] = np.asarray(inputs["L0_attn_b_dt"]).astype(np.float32)
    ppc[:, GSC:GSC + 64] = np.asarray(inputs["L0_ffn_gate_scale"]).astype(np.float32)\
        .reshape(KIN, 128).T
    ppc[:, USC:USC + 64] = np.asarray(inputs["L0_ffn_up_scale"]).astype(np.float32)\
        .reshape(KIN, 128).T

    dscale = np.ascontiguousarray(np.broadcast_to(
        np.asarray(inputs["L0_ffn_down_scale"]).astype(np.float32)[None, :],
        (128, D)))

    shared = dict(wgT=wgT, wuT=wuT, wdT=wdT, wixT=wixT, wizT=wizT, wpT=wpT,
                  wdtT=wdtT, woT=woT, ebr=ebr, ident=ident, dscale=dscale)

    in_maps = []
    for c in range(8):
        b, half = c // 2, c % 2
        if half == 1:
            xl = x[b]
        else:
            xl = np.concatenate([x[b, 1024:], x[b, :1024]], axis=0)
        ppc_c = ppc.copy()
        ppc_c[:, PMASK] = float(half)
        m = dict(shared)
        m["x"] = np.ascontiguousarray(xl)
        m["ppc"] = ppc_c
        in_maps.append(m)
    return in_maps


def kernel(**inputs):
    if "nc" not in _CACHE:
        _CACHE["nc"] = build_program()
    nc = _CACHE["nc"]
    in_maps = _host_prep(inputs)
    res = run_bass_kernel_spmd(nc, in_maps, core_ids=list(range(8)))
    out = np.empty((4, 2048, 2048), np.float32)
    for c in range(8):
        b, half = c // 2, c % 2
        out[b, half * 1024:(half + 1) * 1024] = res.results[c]["out"]
    return out


# version 08
# speedup vs baseline: 1.0143x; 1.0143x over previous
import numpy as np
import ml_dtypes
from contextlib import ExitStack

import concourse.bass as bass
import concourse.mybir as mybir
from concourse.bass_utils import run_bass_kernel_spmd

F32 = mybir.dt.float32
BF16 = mybir.dt.bfloat16
I32 = mybir.dt.int32
AF = mybir.ActivationFunctionType
ALU = mybir.AluOpType

D = 2048
HCN = 16
CI = 512
CCN = 4
II = 8192
KIN = 64
NQ = 4
LEXT = 2048
LOWN = 1024
BLK = 512
NBLK = 4
DTR = 128
NH = 32

SBUF_BASE = 16448
SBUF_TOP = 229376 - 128

DMA_ENGS = ("sync", "gp")
PHYS = {"sync": "sync", "gp": "gpsimd", "gpc": "gpsimd", "tensor": "tensor",
        "vector": "vector", "scalar": "scalar"}
DMA_SEM_NAMES = {f"{q}{j}" for q in ("sync", "gp") for j in range(64)}

ALOG, CONVW, CONVB, DVEC, BDT, PMASK, GSC, USC = 0, 4, 20, 24, 28, 29, 30, 94


NDS = 8


class Builder:

    def __init__(self):
        self.ops = []
        self.awaited = set()

    def emit(self, eng, fn, waits=(), inc=None):
        ws = [w for w in waits if w is not None]
        self.ops.append(dict(eng=eng, fn=fn, waits=ws, inc=inc))
        for w in ws:
            self.awaited.add(w)

    def number(self):
        self.evt = {}
        cnt = {}
        dma_idx = {}
        for op in self.ops:
            eng, inc = op["eng"], op["inc"]
            op["self_wait"] = None
            if eng in DMA_ENGS:
                i = dma_idx.get(eng, 0)
                dma_idx[eng] = i + 1
                sem = f"{eng}{i % NDS}"
                if i >= NDS:
                    op["self_wait"] = (sem, 16 * (i // NDS))
                assert inc is not None
                self.evt[inc] = (sem, 16 * (i // NDS + 1))
                op["do_inc"] = True
                op["sem"] = sem
            elif inc is not None and inc in self.awaited:
                n = cnt.get(eng, 0) + 1
                cnt[eng] = n
                self.evt[inc] = (eng, n)
                op["do_inc"] = True
                op["sem"] = eng
            else:
                op["do_inc"] = False
            op["phys"] = PHYS[eng]
        for op in self.ops:
            for w in op["waits"]:
                assert w in self.evt, f"wait on unregistered event {w}"
        globals()["LAST_EVT"] = dict(self.evt)
        self.dma_final = {}
        for eng in DMA_ENGS:
            n = dma_idx.get(eng, 0)
            for j in range(NDS):
                k = (n - j + NDS - 1) // NDS
                if k > 0:
                    self.dma_final[f"{eng}{j}"] = 16 * k

    def run(self, phys_name, engine, sems, ranges=None, drains=None,
            collect=None):
        maxw = {}

        def wait(sem, val):
            if maxw.get(sem, -1) >= val:
                return
            maxw[sem] = val
            engine.wait_ge(sems[sem], val)

        idx = 0
        for op in self.ops:
            if op["phys"] != phys_name:
                continue
            for w in op["waits"]:
                sem, val = self.evt[w]
                if sem not in DMA_SEM_NAMES and PHYS.get(sem) == phys_name:
                    continue
                wait(sem, val)
            if op["self_wait"] is not None:
                wait(*op["self_wait"])
            if drains is not None and (phys_name, idx) in drains:
                engine.drain()
            instr = op["fn"](engine)
            if op["do_inc"]:
                instr.then_inc(sems[op["sem"]],
                               16 if op["eng"] in DMA_ENGS else 1)
            if collect is not None and instr is not None:
                collect.setdefault(phys_name, []).append(
                    _rw_ranges(instr.ins, ranges))
            idx += 1


def _rw_ranges(mi, ranges):
    def conv(ap_list):
        out = []
        for x in ap_list:
            memref = getattr(x, "memref", None)
            if memref is None:
                continue
            off = x.offset
            span = 1
            for step, count in list(x.ap)[1:]:
                span += abs(step) * (count - 1)
            sz = mybir.dt.size(x.dtype)
            base = ranges.get(memref)
            if base is not None:
                out.append(("SB", base[0] + off * sz, base[0] + (off + span) * sz))
            else:
                out.append((memref, off * sz, (off + span) * sz))
        return out
    return conv(mi.ins), conv(mi.outs)


def _overlap(a, b):
    for sa, la, ha in a:
        for sb, lb, hb in b:
            if sa == sb and la < hb and lb < ha:
                return True
    return False


def _drain_set(collect):
    drains = set()
    for eng, seq in collect.items():
        if eng in ("sync", "tensor"):
            continue
        for i in range(1, len(seq)):
            r, w = seq[i]
            for j in (i - 1, i - 2):
                if j < 0:
                    break
                pr, pw = seq[j]
                if _overlap(w, pw) or _overlap(r, pw) or _overlap(w, pr):
                    drains.add((eng, i))
                    break
                if (eng, j) in drains:
                    break
    return drains


class Mem:
    def __init__(self, nc):
        self.nc = nc
        self.off = SBUF_BASE
        self.peak = SBUF_BASE
        self.ranges = {}

    def alloc(self, name, shape, dtype, at=None):
        size = shape[1] * mybir.dt.size(dtype)
        if at is None:
            at = self.off
            self.off = at + ((size + 63) // 64) * 64
        self.peak = max(self.peak, at + size)
        assert self.peak <= SBUF_TOP, f"SBUF overflow at {name}: {self.peak}"
        h = self.nc.alloc_sbuf_tensor_at(name, list(shape), dtype, offset=at)
        self.ranges[h.name] = (at, at + size)
        return h

    def reserve(self, size):
        at = self.off
        self.off = at + ((size + 63) // 64) * 64
        self.peak = max(self.peak, self.off)
        assert self.peak <= SBUF_TOP
        return at


DEBUG_TAPS = []
TAP_POINTS = set()
TAPS = {}
BUILD_LIMIT = None
MAMBA_STAGE = None
XP_VARIANT = "full"
N_MM1 = 16
MM1_RHS = "xnT"
MM1_EVAC = True
MM1_PS128 = False
NO_TRANSPOSE = False


def _phase_on(name):
    order = ["startup", "mamba", "gateup", "down", "tail"]
    if BUILD_LIMIT is None:
        return True
    return order.index(name) <= order.index(BUILD_LIMIT)


def _mstage_on(name):
    order = ["ln", "xp", "z", "conv", "params", "scan", "y"]
    if MAMBA_STAGE is None:
        return True
    return order.index(name) <= order.index(MAMBA_STAGE)


def build_program():
    collect = {}
    _build(collect=collect, drains=None)
    drains = _drain_set(collect)
    return _build(collect=None, drains=drains)


def _build(collect, drains):
    nc = bass.Bass("TRN2", target_bir_lowering=False,
                   detect_race_conditions=False)

    x_d = nc.dram_tensor("x", [LEXT, D], F32, kind="ExternalInput").ap()
    wg_d = nc.dram_tensor("wgT", [KIN * 128, D], I32, kind="ExternalInput").ap()
    wu_d = nc.dram_tensor("wuT", [KIN * 128, D], I32, kind="ExternalInput").ap()
    wd_d = nc.dram_tensor("wdT", [4 * KIN * 128, 512], I32, kind="ExternalInput").ap()
    wix_d = nc.dram_tensor("wixT", [D, CI], F32, kind="ExternalInput").ap()
    wiz_d = nc.dram_tensor("wizT", [D, CI], F32, kind="ExternalInput").ap()
    wp_d = nc.dram_tensor("wpT", [CI, 1152], F32, kind="ExternalInput").ap()
    wdt_d = nc.dram_tensor("wdtT", [DTR, NH], F32, kind="ExternalInput").ap()
    wo_d = nc.dram_tensor("woT", [CI, D], F32, kind="ExternalInput").ap()
    ebr_d = nc.dram_tensor("ebr", [NH, CI], F32, kind="ExternalInput").ap()
    id_d = nc.dram_tensor("ident", [128, 128], BF16, kind="ExternalInput").ap()
    ppc_d = nc.dram_tensor("ppc", [128, 158], F32, kind="ExternalInput").ap()
    dsc_d = nc.dram_tensor("dscale", [128, D], F32, kind="ExternalInput").ap()
    out_d = nc.dram_tensor("out", [LOWN, D], F32, kind="ExternalOutput").ap()

    B = Builder()

    with ExitStack() as st:
        mem = Mem(nc)
        al = mem.alloc

        xnT = al("xnT", [128, HCN * 1024], BF16)
        gTok = al("gTok", [128, 8 * D], BF16)
        yT = al("yT", [128, CCN * 1024], BF16)
        ident = al("identS", [128, 128], BF16)
        ebr = al("ebrS", [128, CI], F32)
        ppc = al("ppcS", [128, 158], F32)
        aneg = al("anegS", [128, CCN], F32)
        hcarry = al("hcarryS", [128, CCN], F32)
        wp_bf = al("wp_bf", [128, CCN * 1152], BF16)
        wdt_bf = al("wdt_bf", [128, NH], BF16)
        msum = al("msum", [128, 1], F32)
        ssum = al("ssum", [128, 1], F32)
        meanb = al("meanb", [128, 1], F32)
        msq = al("msq", [128, 1], F32)
        varb = al("varb", [128, 1], F32)
        stdb = al("stdb", [128, 1], F32)
        rstd = al("rstd", [128, 1], F32)
        acc1 = al("acc1", [128, 1], F32)
        acc2 = al("acc2", [128, 1], F32)
        negmr = al("negmr", [128, 1], F32)

        g1 = mem.reserve(32768)
        stage32 = al("stage32", [128, 8192], F32, at=g1)
        hT = al("hT", [128, 16 * 1024], BF16, at=g1)

        g2 = mem.reserve(24576)
        xt = [al(f"xt{i}", [128, D], F32, at=g2 + i * 8192) for i in range(2)]
        xn_tok = al("xn_tok", [128, D], BF16, at=g2 + 16384)
        sqh = al("sqh", [128, 1024], F32, at=g2 + 16384 + 4096)
        i32g = [al(f"i32g{i}", [128, 1024], I32, at=g2 + i * 4096) for i in range(3)]
        i32u = [al(f"i32u{i}", [128, 1024], I32, at=g2 + 12288 + i * 4096) for i in range(3)]
        xres = [al(f"xres{i}", [128, 1024], F32, at=g2 + i * 4096) for i in range(2)]
        otile = [al(f"otile{i}", [128, D], F32, at=g2 + 8192 + i * 8192) for i in range(2)]
        wostg = [al(f"wostg{i}", [128, D], F32, at=g2 + i * 8192) for i in range(2)]

        g3 = mem.reserve(16384)
        xnTblk = al("xnTblk", [128, HCN * 512], BF16, at=g3)
        wiz_bf = al("wiz_bf", [128, HCN * 512], BF16, at=g3)
        bfg = [al(f"bfg{i}", [128, 1024], BF16, at=g3 + i * 2048) for i in range(3)]
        bfu = [al(f"bfu{i}", [128, 1024], BF16, at=g3 + 6144 + i * 2048) for i in range(3)]
        wo_bf = al("wo_bf", [128, CCN * D], BF16, at=g3)

        G4SZ = 32768
        g4 = mem.reserve(G4SZ)
        o = g4
        xp = al("xp", [128, CCN * 520], BF16, at=o); o += 4160
        convacc = al("convacc", [128, 512], F32, at=o); o += 2048
        x_act = al("x_act", [128, CCN * 512], BF16, at=o); o += 4096
        dt_blk = al("dt_blk", [128, 512], BF16, at=o); o += 1024
        b_blk = al("b_blk", [128, 512], BF16, at=o); o += 1024
        c_blk = al("c_blk", [128, 512], BF16, at=o); o += 1024
        spe = al("spe", [128, 512], F32, at=o); o += 2048
        delta = al("delta", [128, 512], F32, at=o); o += 2048
        a_blk = al("a_blk", [128, 512], F32, at=o); o += 2048
        bu_blk = al("bu_blk", [128, 512], F32, at=o); o += 2048
        h_blk = al("h_blk", [128, 512], F32, at=o); o += 2048
        t1 = al("t1", [128, 512], F32, at=o); o += 2048
        t1b = al("t1b", [128, 512], F32, at=o); o += 2048
        siluz = al("siluz", [128, CCN * 512], BF16, at=o); o += 4096
        assert o - g4 <= G4SZ, o - g4
        o = g4
        tsilu = [al(f"tsilu{i}", [128, 512], BF16, at=o + i * 1024) for i in range(4)]
        o += 4096
        i32d = [al(f"i32d{i}", [128, 512], I32, at=o + i * 2048) for i in range(2)]
        o += 4096
        bfd = [al(f"bfd{i}", [128, 512], BF16, at=o + i * 1024) for i in range(2)]
        o += 2048
        t5 = al("t5", [128, 512], F32, at=o); o += 2048
        t6 = al("t6", [128, 512], F32, at=o); o += 2048
        t7 = al("t7", [128, 512], F32, at=o); o += 2048
        t5b = al("t5b", [128, 512], F32, at=o); o += 2048
        t6b = al("t6b", [128, 512], F32, at=o); o += 2048
        assert o - g4 <= G4SZ

        wizstg = [al("wizstg0", [128, 512], F32)] * 2
        g5 = mem.reserve(16384)
        wix_bf = al("wix_bf", [128, HCN * 512], BF16, at=g5)
        dscale = al("dscaleS", [128, D], F32, at=g5)

        ps = [st.enter_context(nc.psum_tensor(f"ps{i}", [128, 512], F32))
              for i in range(8)]
        bank_readers = {i: [] for i in range(8)}

        sems = {}
        for name in ("tensor", "vector", "scalar", "gpc"):
            sems[name] = st.enter_context(nc.semaphore(f"sem_{name}"))
        for q in ("sync", "gp"):
            for j in range(NDS):
                sems[f"{q}{j}"] = st.enter_context(nc.semaphore(f"sem_{q}{j}"))

        pe_pending = []

        def pe_mm(fn, ncols, waits=(), inc=None, inc_direct=None):
            for p in pe_pending:
                p[1] -= ncols
            carry = inc_direct
            if carry is None and pe_pending and pe_pending[0][1] <= 0:
                carry = pe_pending.pop(0)[0]
            B.emit("tensor", fn, waits=waits, inc=carry)
            if inc is not None:
                pe_pending.append([inc, 256])

        def pe_flush():
            while pe_pending:
                evt, _ = pe_pending.pop(0)
                B.emit("tensor", (lambda e: e.drain()), inc=evt)

        def mm_group(bank, pairs, out_ap, inc, waits_first=(), waits_at=None,
                     ncols=512):
            n = len(pairs)
            prev = bank_readers[bank]
            bank_readers[bank] = []
            for i, (l, r) in enumerate(pairs):
                w = list(waits_first) + (prev if i == 0 else [])
                if waits_at and i in waits_at:
                    w = w + waits_at[i]
                pe_mm((lambda e, l=l, r=r, s=(i == 0), p=(i == n - 1), oa=out_ap:
                       e.matmul(oa, l, r, start=s, stop=p)),
                      ncols, waits=w, inc=inc if i == n - 1 else None)

        def read_bank(bank, evt):
            bank_readers[bank].append(evt)

        def tap(eng, label, handle):
            if label not in TAP_POINTS:
                return
            import concourse.bass_interp as BI

            def fn(e, h=handle, label=label):
                def snap(core, h=h, label=label):
                    TAPS[label] = np.asarray(core.tensor(h.name)).copy()
                return BI.add_callback(e, snap)
            B.emit(eng, fn)

        B.emit("sync", lambda e: e.dma_start(ident[:], id_d), inc="d_ident")
        B.emit("sync", lambda e: e.dma_start(ebr[0:NH, :], ebr_d), inc="d_ebr")
        B.emit("sync", lambda e: e.dma_start(ppc[:], ppc_d), inc="d_ppc")
        B.emit("scalar", lambda e: e.activation(aneg[:], ppc[:, ALOG:ALOG + 4], AF.Exp),
               waits=["d_ppc"], inc="s_aneg0")
        B.emit("vector", lambda e: e.tensor_scalar_mul(aneg[:], aneg[:], -1.0),
               waits=["s_aneg0"], inc="v_aneg")
        for c in range(HCN):
            B.emit("sync", (lambda e, c=c: e.dma_start(
                stage32[:, c * 512:(c + 1) * 512], wix_d[c * 128:(c + 1) * 128, :])),
                inc=f"d_wix{c}")
        B.emit("vector", lambda e: e.tensor_copy(wix_bf[:], stage32[:]),
               waits=[f"d_wix{c}" for c in range(HCN - NDS, HCN)], inc="v_cwix")
        for c in range(CCN):
            B.emit("sync", (lambda e, c=c: e.dma_start(
                stage32[:, c * 1152:(c + 1) * 1152], wp_d[c * 128:(c + 1) * 128, :])),
                waits=["v_cwix"] if c == 0 else (), inc=f"d_wp{c}")
        B.emit("vector", lambda e: e.tensor_copy(wp_bf[:], stage32[:, 0:CCN * 1152]),
               waits=[f"d_wp{c}" for c in range(CCN)], inc="v_cwp")
        B.emit("sync", lambda e: e.dma_start(stage32[:, 4608:4640], wdt_d),
               waits=["v_cwp"], inc="d_wdt")
        B.emit("vector", lambda e: e.tensor_copy(wdt_bf[:], stage32[:, 4608:4640]),
               waits=["d_wdt"], inc="v_cwdt")

        gtok_at = mem.ranges[gTok.name][0]
        i32e = [al(f"i32e{i}", [128, 1024], I32, at=gtok_at + i * 4096)
                for i in range(4)]
        bfge = [al(f"bfge{i}", [128, 1024], BF16, at=gtok_at + 16384 + i * 2048)
                for i in range(3)]
        bfue = [al(f"bfue{i}", [128, 1024], BF16, at=gtok_at + 22528 + i * 2048)
                for i in range(3)]
        tsie = [al(f"tsie{i}", [128, 512], BF16, at=gtok_at + 28672 + i * 1024)
                for i in range(4)]

        zmm_last = None
        ln_done_evts = []
        blk_trdone = {}
        hst_evts = {}
        ffn_k = [0]
        load_k = [0]
        feed_cap = [99]
        ln_order = [8, 9, 10, 11, 12, 13, 14, 15, 0, 1, 2, 3, 4, 5, 6, 7]
        ln_pos = {g: s for s, g in enumerate(ln_order)}

        def feed_load(k):
            if k >= 16:
                return
            for wi, which in enumerate(("g", "u")):
                src_d = wg_d if which == "g" else wu_d
                bfl = bfge if which == "g" else bfue
                for half in range(2):
                    gi = 2 * k + half
                    islot = wi * 2 + half
                    wt = []
                    if k >= 1:
                        wt.append(f"s_c{which}{2*(k-1)+half}")
                    B.emit("sync", (lambda e, src_d=src_d, dst=i32e[islot], k=k,
                                    lo=half * 1024: e.dma_start(
                        dst[:], src_d[k * 128:(k + 1) * 128, lo:lo + 1024])),
                        waits=wt, inc=f"d_{which}{gi}")
                    cwt = [f"d_{which}{gi}"]
                    if gi >= 3:
                        cwt.append(f"t_gu{(gi-3)//2}_{which}1")
                    B.emit("scalar", (lambda e, bfs=bfl[gi % 3],
                                      i32s=i32e[islot]: e.activation(
                        bfs[:], i32s[:], AF.Copy, bias=-128.0)),
                        waits=cwt, inc=f"s_c{which}{gi}")

        def ffn_feed(nk):
            if not _phase_on("gateup"):
                return
            for _ in range(nk):
                if feed_cap[0] <= 0:
                    return
                feed_cap[0] -= 1
                k = ffn_k[0]
                if k >= 16:
                    return
                ffn_k[0] = k + 1
                pe_flush()
                while load_k[0] <= min(k + 1, 15):
                    feed_load(load_k[0])
                    load_k[0] += 1
                for which in ("g", "u"):
                    wslots = bfge if which == "g" else bfue
                    for tch in range(2):
                        bank = 4 + (0 if which == "g" else 2) + tch
                        pairs = []
                        for c in range(HCN):
                            half = c // 8
                            sl = wslots[(2 * k + half) % 3]
                            pairs.append((sl[:, (c % 8) * 128:(c % 8 + 1) * 128],
                                          xnT[:, c * 1024 + tch * 512: c * 1024 + tch * 512 + 512]))
                        wa = {0: [f"s_c{which}{2*k}"], 8: [f"s_c{which}{2*k+1}"]}
                        if k == 0:
                            wa[0] = wa[0] + blk_trdone[2] + blk_trdone[3]
                        mm_group(bank, pairs, ps[bank][:],
                                 inc=f"t_gu{k}_{which}{tch}", waits_at=wa)
                for tch in range(2):
                    gbank = 4 + tch
                    ubank = 6 + tch
                    tslot = (k % 2) * 2 + tch
                    sw = [f"t_gu{k}_g{tch}"]
                    if k >= 2:
                        sw.append(f"v_h{k-2}_{tch}")
                    B.emit("scalar", (lambda e, k=k, tslot=tslot, gbank=gbank:
                           e.activation(
                        tsie[tslot][:], ps[gbank][:], AF.Silu,
                        scale=ppc[:, GSC + k:GSC + k + 1])),
                        waits=sw, inc=f"s_sil{k}_{tch}")
                    read_bank(gbank, f"s_sil{k}_{tch}")
                    hw = [f"t_gu{k}_u{tch}", f"s_sil{k}_{tch}"]
                    if k == 0:
                        hw.append("v_cwdt")
                    B.emit("vector", (lambda e, k=k, tch=tch, tslot=tslot,
                                      ubank=ubank: e.scalar_tensor_tensor(
                        hT[:, (k % 16) * 1024 + tch * 512:(k % 16) * 1024 + tch * 512 + 512],
                        ps[ubank][:], ppc[:, USC + k:USC + k + 1],
                        tsie[tslot][:], ALU.mult, ALU.mult)),
                        waits=hw, inc=f"v_h{k}_{tch}")
                    read_bank(ubank, f"v_h{k}_{tch}")
                    hst_evts[(k, tch)] = f"v_h{k}_{tch}"
                pe_flush()

        def emit_ln_tile(g, fw):
            s = ln_pos[g]
            b, j = g // 4, g % 4
            p1 = ln_order[s - 1] if s >= 1 else None
            p2 = ln_order[s - 2] if s >= 2 else None
            slot = s % 2
            if b >= 2:
                def dst(c, b=b, j=j):
                    lo = c * 1024 + (b - 2) * 512 + j * 128
                    return xnT[:, lo:lo + 128]
            else:
                def dst(c, j=j):
                    return xnTblk[:, c * 512 + j * 128: c * 512 + j * 128 + 128]
            wt = []
            if p2 is not None:
                wt = [f"s_norm{p2}", f"v_red{p2}"]
            B.emit("gp", (lambda e, g=g, slot=slot: e.dma_start(
                xt[slot][:], x_d[g * 128:(g + 1) * 128, :])),
                waits=wt, inc=f"d_x{g}")
            sq_w = [f"d_x{g}"] + fw("scalar")
            if p1 is not None:
                sq_w += [f"v_red{p1}"]
            B.emit("scalar", (lambda e, slot=slot: e.activation(
                sqh[:], xt[slot][:, 0:1024], AF.Square, accum_out=acc1[:])),
                waits=sq_w, inc=None)
            B.emit("scalar", (lambda e, slot=slot: e.activation(
                sqh[:], xt[slot][:, 1024:2048], AF.Square, accum_out=acc2[:])),
                inc=f"s_sq{g}")
            B.emit("vector", (lambda e, slot=slot: e.reduce_sum(
                msum[:], xt[slot][:], axis=mybir.AxisListType.X)),
                waits=[f"d_x{g}"] + fw("vector"))
            B.emit("vector", (lambda e: e.tensor_add(
                ssum[:], acc1[:], acc2[:])),
                waits=[f"s_sq{g}"], inc=f"v_red{g}")
            B.emit("vector", lambda e: e.tensor_scalar_mul(meanb[:], msum[:], 1.0 / D),
                   waits=[f"s_norm{p1}"] if p1 is not None else [])
            B.emit("vector", lambda e: e.tensor_scalar(
                msq[:], meanb[:], meanb[:], 1e-5, ALU.mult, ALU.subtract))
            B.emit("vector", lambda e: e.tensor_scalar(
                varb[:], ssum[:], 1.0 / D, msq[:], ALU.mult, ALU.subtract))
            B.emit("vector", lambda e: e.tensor_scalar_mul(negmr[:], meanb[:], -1.0),
                   inc=f"v_lnt{g}")
            B.emit("scalar", lambda e: e.activation(stdb[:], varb[:], AF.Sqrt),
                   waits=[f"v_lnt{g}"], inc=f"s_std{g}")
            B.emit("vector", lambda e: e.reciprocal(rstd[:], stdb[:]),
                   waits=[f"s_std{g}"])
            B.emit("vector", (lambda e, slot=slot: e.tensor_scalar(
                xn_tok[:], xt[slot][:], negmr[:], rstd[:], ALU.add, ALU.mult)),
                waits=[f"t_tr{p1}_{HCN-1}"] if p1 is not None else [],
                inc=f"s_norm{g}")
            for c in range(HCN):
                bank = c % 4
                prev = bank_readers[bank]
                bank_readers[bank] = []
                pe_mm((lambda e, c=c, bank=bank: e.matmul(
                    ps[bank][:, 0:128], xn_tok[:, c * 128:(c + 1) * 128],
                    ident[:], start=True, stop=True)), 128,
                    waits=[f"s_norm{g}", "d_ident"] + prev + fw("tensor"),
                    inc=f"t_tr{g}_{c}")
                B.emit("vector", (lambda e, c=c, bank=bank, dst=dst:
                       e.tensor_copy(dst(c), ps[bank][:, 0:128])),
                       waits=[f"t_tr{g}_{c}"], inc=f"v_tr{g}_{c}")
                read_bank(bank, f"v_tr{g}_{c}")
            pe_flush()

        for si, g in enumerate(ln_order[:8]):
            emit_ln_tile(g, lambda eng: [])
            if si == 5 and _phase_on("gateup"):
                feed_load(0)
                load_k[0] = 1
        for b in (2, 3):
            blk_trdone[b] = [f"v_tr{b*4+j}_{c}" for j in range(4)
                             for c in range(HCN)]
        ffn_feed(3)

        for b in range(NBLK):
            own = b >= 2
            if own:
                def xcols(c, b=b):
                    lo = c * 1024 + (b - 2) * 512
                    return xnT[:, lo:lo + 512]
            else:
                def xcols(c):
                    return xnTblk[:, c * 512:(c + 1) * 512]

            if b >= 1:
                first_w = {"scalar": [f"bar_v{b-1}", f"bar_t{b-1}"],
                           "vector": [f"bar_s{b-1}", f"bar_t{b-1}"],
                           "tensor": [f"bar_s{b-1}", f"bar_v{b-1}"]}
            else:
                first_w = {"scalar": [], "vector": [], "tensor": []}
            fw_used = {k: False for k in first_w}

            def fw(eng):
                if fw_used[eng]:
                    return []
                fw_used[eng] = True
                return first_w[eng]

            feed_cap[0] = {0: 3, 1: 4, 2: 5, 3: 8}[b]
            if not own:
                for j in range(4):
                    emit_ln_tile(b * 4 + j, fw)
                blk_trdone[b] = [f"v_tr{b*4+j}_{c}" for j in range(4)
                                 for c in range(HCN)]
                if b == 1:
                    g = 7
                    ln_done_evts += [f"s_norm{g}", f"v_red{g}",
                                     f"t_tr{g}_{HCN-1}"]
            trdone = blk_trdone[b]
            ffn_feed(1)

            for cc in range(CCN if (_mstage_on("xp") and XP_VARIANT == "full") else 0):
                last = (cc == CCN - 1 and b != 2)
                if b == 0:
                    B.emit("vector", (lambda e, cc=cc: e.memset(
                        xp[:, cc * 520:cc * 520 + 3], 0.0)),
                        inc="v_carry0" if last else None)
                else:
                    B.emit("vector", (lambda e, cc=cc: e.tensor_copy(
                        xp[:, cc * 520:cc * 520 + 3],
                        xp[:, cc * 520 + 512:cc * 520 + 515])),
                        waits=fw("vector"),
                        inc=f"v_carry{b}" if last else None)
                    if b == 2:
                        B.emit("vector", (lambda e, cc=cc: e.tensor_scalar_mul(
                            xp[:, cc * 520:cc * 520 + 3],
                            xp[:, cc * 520:cc * 520 + 3],
                            ppc[:, PMASK:PMASK + 1])),
                            inc=f"v_carry{b}" if cc == CCN - 1 else None)

            for cc in range(CCN if _mstage_on("xp") else 0):
                bank = 0 + (cc % 2)
                pairs = [(wix_bf[:, c * 512 + cc * 128: c * 512 + (cc + 1) * 128],
                          xcols(c)) for c in range(HCN)]
                mm_group(bank, pairs, ps[bank][:], inc=f"t_xp{b}_{cc}",
                         waits_first=(["v_cwix"] + trdone + fw("tensor"))
                         if cc == 0 else [])
                B.emit("vector", (lambda e, cc=cc, bank=bank: e.tensor_copy(
                    xp[:, cc * 520 + 3:cc * 520 + 515], ps[bank][:])),
                    waits=[f"t_xp{b}_{cc}", f"v_carry{b}"], inc=f"v_xp{b}_{cc}")
                read_bank(bank, f"v_xp{b}_{cc}")
            if _mstage_on("xp"):
                pe_flush()
            if b == 1 and _mstage_on("z"):
                for c in range(HCN):
                    wwt = [f"t_xp1_{CCN-1}"] if c < 1 else [f"v_wz{c-1}"]
                    B.emit("sync", (lambda e, c=c: e.dma_start(
                        wizstg[c % 2][:], wiz_d[c * 128:(c + 1) * 128, :])),
                        waits=wwt, inc=f"d_wiz{c}")
                    B.emit("vector", (lambda e, c=c: e.tensor_copy(
                        wiz_bf[:, c * 512:(c + 1) * 512], wizstg[c % 2][:])),
                        waits=[f"d_wiz{c}"], inc=f"v_wz{c}")
                B.emit("vector", lambda e: e.nop(), inc="v_cwiz")
            ffn_feed(1)

            if own and _mstage_on("z"):
                for cc in range(CCN):
                    bank = 2 + (cc % 2)
                    pairs = [(wiz_bf[:, c * 512 + cc * 128: c * 512 + (cc + 1) * 128],
                              xcols(c)) for c in range(HCN)]
                    mm_group(bank, pairs, ps[bank][:], inc=f"t_z{b}_{cc}",
                             waits_first=["v_cwiz"] if (b == 2 and cc == 0) else [])
                    B.emit("scalar", (lambda e, cc=cc, bank=bank: e.activation(
                        siluz[:, cc * 512:(cc + 1) * 512], ps[bank][:], AF.Silu)),
                        waits=[f"t_z{b}_{cc}"] + fw("scalar"), inc=f"s_z{b}_{cc}")
                    read_bank(bank, f"s_z{b}_{cc}")
                    zmm_last = f"t_z{b}_{cc}"
                pe_flush()

            for cc in range(CCN if _mstage_on("conv") else 0):
                base = cc * 520
                B.emit("vector", (lambda e, base=base, cc=cc: e.tensor_scalar_mul(
                    convacc[:], xp[:, base:base + 512],
                    ppc[:, CONVW + cc * 4:CONVW + cc * 4 + 1])),
                    waits=[f"v_xp{b}_{cc}"]
                    + ([f"s_xact{b}_{cc-1}"] if cc >= 1 else []))
                for k in range(1, 4):
                    B.emit("vector", (lambda e, base=base, cc=cc, k=k:
                           e.scalar_tensor_tensor(
                               convacc[:], xp[:, base + k:base + 512 + k],
                               ppc[:, CONVW + cc * 4 + k:CONVW + cc * 4 + k + 1],
                               convacc[:], ALU.mult, ALU.add)),
                           inc=f"v_conv{b}_{cc}" if k == 3 else None)
                B.emit("scalar", (lambda e, cc=cc: e.activation(
                    x_act[:, cc * 512:(cc + 1) * 512], convacc[:], AF.Silu,
                    bias=ppc[:, CONVB + cc:CONVB + cc + 1])),
                    waits=[f"v_conv{b}_{cc}"], inc=f"s_xact{b}_{cc}")
                tap("scalar", f"xact{b}_{cc}", x_act)
                if cc == 1 or cc == 3:
                    ffn_feed(1)

            xact_w = [f"s_xact{b}_{cc}" for cc in range(CCN)]

            if not _mstage_on("params"):
                pe_flush()
                B.emit("tensor", lambda e: e.nop(), inc=f"bar_t{b}")
                B.emit("vector", lambda e: e.nop(), inc=f"bar_v{b}")
                B.emit("scalar", lambda e: e.nop(), inc=f"bar_s{b}")
                continue
            bank = 0
            pairs = [(wp_bf[:, cc * 1152: cc * 1152 + 128],
                      x_act[:, cc * 512:(cc + 1) * 512]) for cc in range(CCN)]
            mm_group(bank, pairs, ps[bank][:], inc=f"t_dt{b}",
                     waits_first=xact_w + ["v_cwp"])
            pe_flush()
            B.emit("vector", (lambda e: e.tensor_copy(dt_blk[:], ps[0][:])),
                   waits=[f"t_dt{b}"], inc=f"v_dt{b}")
            tap("vector", f"dt{b}", dt_blk)
            read_bank(0, f"v_dt{b}")
            mm_group(1, [(wdt_bf[:], dt_blk[:])], ps[1][0:NH, :],
                     inc=f"t_del{b}", waits_first=[f"v_dt{b}", "v_cwdt"])
            pe_flush()
            B.emit("scalar", (lambda e: e.activation(
                spe[0:NH, :], ps[1][0:NH, :], AF.Exp, bias=ppc[0:NH, BDT:BDT + 1])),
                waits=[f"t_del{b}"], inc=f"s_spe{b}")
            read_bank(1, f"s_spe{b}")
            B.emit("scalar", lambda e: e.activation(
                delta[0:NH, :], spe[0:NH, :], AF.Ln, bias=1.0), inc=f"s_del{b}")
            tap("scalar", f"delta{b}", delta)
            ffn_feed(1)

            for q in range(CCN if _mstage_on("scan") else 0):
                bank = 0 + (q % 2)
                pairs = [(wp_bf[:, cc * 1152 + (1 + q) * 128: cc * 1152 + (2 + q) * 128],
                          x_act[:, cc * 512:(cc + 1) * 512]) for cc in range(CCN)]
                mm_group(bank, pairs, ps[bank][:], inc=f"t_pb{b}_{q}")
                B.emit("vector", (lambda e, bank=bank: e.tensor_copy(
                    b_blk[:], ps[bank][:])),
                    waits=[f"t_pb{b}_{q}"], inc=f"v_pb{b}_{q}")
                tap("vector", f"b{b}_{q}", b_blk)
                read_bank(bank, f"v_pb{b}_{q}")
                if own:
                    bank = 2 + (q % 2)
                    pairs = [(wp_bf[:, cc * 1152 + (5 + q) * 128: cc * 1152 + (6 + q) * 128],
                              x_act[:, cc * 512:(cc + 1) * 512]) for cc in range(CCN)]
                    mm_group(bank, pairs, ps[bank][:], inc=f"t_pc{b}_{q}")
                    B.emit("vector", (lambda e, bank=bank: e.tensor_copy(
                        c_blk[:], ps[bank][:])),
                        waits=[f"t_pc{b}_{q}"], inc=f"v_pc{b}_{q}")
                    read_bank(bank, f"v_pc{b}_{q}")
                bcb = (0 if own else 2) + (q % 2)
                mm_group(bcb, [(ebr[0:NH, q * 128:(q + 1) * 128], delta[0:NH, :])],
                         ps[bcb][:], inc=f"t_bc{b}_{q}",
                         waits_first=[f"s_del{b}", "d_ebr"])
                scal = 1.0 if own else ppc[:, PMASK:PMASK + 1]
                B.emit("vector", (lambda e, bcb=bcb, scal=scal: e.scalar_tensor_tensor(
                    t1[:], ps[bcb][:], scal, b_blk[:], ALU.mult, ALU.mult)),
                    waits=[f"t_bc{b}_{q}", f"v_pb{b}_{q}"], inc=f"v_t1{b}_{q}")
                B.emit("vector", (lambda e, q=q: e.tensor_mul(
                    bu_blk[:], t1[:], x_act[:, q * 512:(q + 1) * 512])),
                    inc=f"v_bu{b}_{q}")
                tap("vector", f"bu{b}_{q}", bu_blk)
                tap("vector", f"t1_{b}_{q}", t1)
                aw = [f"v_t1{b}_{q}", "v_aneg"]
                if q > 0:
                    aw.append(f"v_scan{b}_{q-1}")
                elif b > 0:
                    aw.append(f"bar_v{b-1}")
                B.emit("scalar", (lambda e, q=q, bcb=bcb: e.activation(
                    a_blk[:], ps[bcb][:], AF.Exp, scale=aneg[:, q:q + 1])),
                    waits=aw, inc=f"s_a{b}_{q}")
                read_bank(bcb, f"s_a{b}_{q}")
                init = 0.0 if b == 0 else hcarry[:, q:q + 1]
                B.emit("vector", (lambda e, init=init: e.tensor_tensor_scan(
                    h_blk[:], a_blk[:], bu_blk[:], init, ALU.mult, ALU.add)),
                    waits=[f"s_a{b}_{q}"], inc=f"v_scan{b}_{q}")
                B.emit("vector", (lambda e, q=q: e.tensor_copy(
                    hcarry[:, q:q + 1], h_blk[:, 511:512])))
                ffn_feed(1)
                if own:
                    B.emit("vector", (lambda e: e.tensor_mul(
                        t1[:], h_blk[:], c_blk[:])),
                        waits=[f"v_pc{b}_{q}"])
                    B.emit("vector", (lambda e, q=q: e.scalar_tensor_tensor(
                        t1b[:], x_act[:, q * 512:(q + 1) * 512],
                        ppc[:, DVEC + q:DVEC + q + 1], t1[:], ALU.mult, ALU.add)))
                    B.emit("vector", (lambda e, q=q, b=b: e.tensor_mul(
                        yT[:, q * 1024 + (b - 2) * 512: q * 1024 + (b - 2) * 512 + 512],
                        t1b[:], siluz[:, q * 512:(q + 1) * 512])),
                        waits=[f"s_z{b}_{q}"], inc=f"v_y{b}_{q}")

            pe_flush()
            B.emit("tensor", lambda e: e.nop(), inc=f"bar_t{b}")
            B.emit("vector", lambda e: e.nop(), inc=f"bar_v{b}")
            B.emit("scalar", lambda e: e.nop(), inc=f"bar_s{b}")

        mamba_done = [f"bar_t{NBLK-1}", f"bar_v{NBLK-1}", f"bar_s{NBLK-1}"]
        feed_cap[0] = 99
        ffn_feed(16)

        B.emit("sync", lambda e: e.dma_start(dscale[:], dsc_d),
               waits=[f"t_xp{NBLK-1}_{CCN-1}"]
               if (_mstage_on("xp") and XP_VARIANT not in ("mm1",)) else [],
               inc="d_dsc")

        down_done = []
        for q4 in range(NQ if _phase_on("gateup") else 0):
            for kk in range(16 if q4 >= 1 else 0):
                k = q4 * 16 + kk
                for which in ("g", "u"):
                    src = wg_d if which == "g" else wu_d
                    i32s_l = i32g if which == "g" else i32u
                    bfs_l = bfg if which == "g" else bfu
                    for half in range(2):
                        gi = 2 * k + half
                        slot = gi % 3
                        wt = []
                        if k == 16 and half == 0 and which == "g":
                            wt = list(ln_done_evts)
                        if gi >= 3:
                            wt.append(f"s_c{which}{gi-3}")
                        B.emit("sync", (lambda e, src=src, dst=i32s_l[slot], k=k,
                                        lo=half * 1024: e.dma_start(
                            dst[:], src[k * 128:(k + 1) * 128, lo:lo + 1024])),
                            waits=wt, inc=f"d_{which}{gi}")
                        cwt = [f"d_{which}{gi}"]
                        if k == 16 and half == 0:
                            cwt.append(zmm_last)
                        if gi >= 3:
                            cwt.append(f"t_gu{(gi-3)//2}_{which}1")
                        B.emit("scalar", (lambda e, bfs=bfs_l[slot], i32s=i32s_l[slot]:
                               e.activation(bfs[:], i32s[:], AF.Copy, bias=-128.0)),
                            waits=cwt, inc=f"s_c{which}{gi}")
                for which in ("g", "u"):
                    wslots = bfg if which == "g" else bfu
                    for tch in range(2):
                        bank = (k % 2) * 4 + (0 if which == "g" else 2) + tch
                        pairs = []
                        for c in range(HCN):
                            half = c // 8
                            sl = wslots[(2 * k + half) % 3]
                            pairs.append((sl[:, (c % 8) * 128:(c % 8 + 1) * 128],
                                          xnT[:, c * 1024 + tch * 512: c * 1024 + tch * 512 + 512]))
                        wa = {0: [f"s_c{which}{2*k}"] + (mamba_done if k == 16 else []),
                              8: [f"s_c{which}{2*k+1}"]}
                        mm_group(bank, pairs, ps[bank][:],
                                 inc=f"t_gu{k}_{which}{tch}", waits_at=wa)
                for tch in range(2):
                    gbank = (k % 2) * 4 + tch
                    ubank = (k % 2) * 4 + 2 + tch
                    tslot = (k % 2) * 2 + tch
                    sw = [f"t_gu{k}_g{tch}"]
                    if q4 > 0 and kk == 0 and down_done:
                        sw.append(down_done[-1])
                    if k >= 2:
                        sw.append(f"v_h{k-2}_{tch}")
                    B.emit("scalar", (lambda e, k=k, tslot=tslot, gbank=gbank:
                           e.activation(
                        tsilu[tslot][:], ps[gbank][:], AF.Silu,
                        scale=ppc[:, GSC + k:GSC + k + 1])),
                        waits=sw, inc=f"s_sil{k}_{tch}")
                    read_bank(gbank, f"s_sil{k}_{tch}")
                    hw = [f"t_gu{k}_u{tch}", f"s_sil{k}_{tch}"]
                    if q4 > 0 and down_done:
                        hw.append(down_done[-1])
                    B.emit("vector", (lambda e, k=k, tch=tch, tslot=tslot, ubank=ubank:
                           e.scalar_tensor_tensor(
                               hT[:, (k % 16) * 1024 + tch * 512:(k % 16) * 1024 + tch * 512 + 512],
                               ps[ubank][:], ppc[:, USC + k:USC + k + 1],
                               tsilu[tslot][:], ALU.mult, ALU.mult)),
                           waits=hw, inc=f"v_h{k}_{tch}")
                    read_bank(ubank, f"v_h{k}_{tch}")
                    hst_evts[(k, tch)] = f"v_h{k}_{tch}"

            pe_flush()
            if q4 == 3 and _phase_on("tail"):
                for c in range(CCN):
                    B.emit("sync", (lambda e, c=c: e.dma_start(
                        wostg[c % 2][:], wo_d[c * 128:(c + 1) * 128, :])),
                        waits=[f"s_cg{2*KIN-1}", f"s_cu{2*KIN-1}"]
                        + ([f"v_cwo{c-2}"] if c >= 2 else []),
                        inc=f"d_wo{c}")
                    B.emit("vector", (lambda e, c=c: e.tensor_copy(
                        wo_bf[:, c * D:(c + 1) * D], wostg[c % 2][:])),
                        waits=[f"d_wo{c}", f"t_gu{KIN-1}_u1"],
                        inc=f"v_cwo{c}")
            if not _phase_on("down"):
                continue
            hT_ready = [hst_evts[(q4 * 16 + kk, tch)] for kk in range(16) for tch in range(2)]
            for hs in range(4):
                for kk in range(16):
                    gidx = (q4 * 4 + hs) * 16 + kk
                    row = (hs * KIN + q4 * 16 + kk) * 128
                    wt = []
                    if gidx < 2:
                        wt = list(mamba_done)
                    else:
                        wt.append(f"s_cd{gidx-2}")
                    B.emit("sync", (lambda e, row=row, sl=gidx % 2: e.dma_start(
                        i32d[sl][:], wd_d[row:row + 128, :])),
                        waits=wt, inc=f"d_wd{gidx}")
                    cwt = [f"d_wd{gidx}"]
                    if gidx < 2:
                        cwt += mamba_done
                    else:
                        p = gidx - 2
                        pq, prem = divmod(p, 64)
                        phs, pkk = divmod(prem, 16)
                        cwt.append(f"t_dn{pq}_{phs}_{pkk}")
                    B.emit("scalar", (lambda e, sl=gidx % 2: e.activation(
                        bfd[sl][:], i32d[sl][:], AF.Copy, bias=-128.0)),
                        waits=cwt, inc=f"s_cd{gidx}")
                    for tc in range(8):
                        prev = bank_readers[tc] if kk == 0 else []
                        if kk == 0:
                            bank_readers[tc] = []
                        w = ([f"s_cd{gidx}"] if tc == 0 else []) + prev
                        if kk == 0 and tc == 0 and hs == 0:
                            w = w + hT_ready
                        pe_mm((lambda e, kk=kk, tc=tc, sl=gidx % 2:
                               e.matmul(ps[tc][:],
                                        hT[:, kk * 1024 + tc * 128: kk * 1024 + tc * 128 + 128],
                                        bfd[sl][:],
                                        start=(kk == 0), stop=(kk == 15))), 512,
                              waits=w,
                              inc_direct=(f"t_dn{q4}_{hs}_{kk}"
                                          if (tc == 7 and kk < 15) else None),
                              inc=(f"t_dn{q4}_{hs}_{kk}"
                                   if (tc == 7 and kk == 15) else None))
                pe_flush()
                for tc in range(8):
                    col = tc * D + hs * 512
                    if q4 == 0:
                        gw = [f"t_dn{q4}_{hs}_15"]
                        if hs == 0 and tc == 0:
                            gw += [f"s_cg{31}", f"s_cu{31}", f"v_h{15}_{1}"]
                        B.emit("vector", (lambda e, col=col, tc=tc: e.tensor_copy(
                            gTok[:, col:col + 512], ps[tc][:])),
                            waits=gw, inc=f"v_dn{q4}_{hs}_{tc}")
                    elif q4 < 3:
                        B.emit("vector", (lambda e, col=col, tc=tc: e.tensor_add(
                            gTok[:, col:col + 512], ps[tc][:], gTok[:, col:col + 512])),
                            waits=[f"t_dn{q4}_{hs}_15"], inc=f"v_dn{q4}_{hs}_{tc}")
                    else:
                        B.emit("vector", (lambda e, col=col, tc=tc: e.tensor_add(
                            gTok[:, col:col + 512], ps[tc][:],
                            gTok[:, col:col + 512])),
                            waits=[f"t_dn{q4}_{hs}_15"], inc=f"v_dn{q4}_{hs}_{tc}")
                    read_bank(tc, f"v_dn{q4}_{hs}_{tc}")
            down_done.append(f"t_dn{q4}_3_15")

        wo_ready = [f"v_cwo{c}" for c in range(CCN)]
        for tc in range(8):
            for hs in range(4):
                col = tc * D + hs * 512
                n8 = tc * 4 + hs
                tb6 = t6 if n8 % 2 == 0 else t6b
                pv = n8 - 2
                gw = [f"v_dn{3}_{hs}_{tc}", "d_dsc"]
                if pv >= 0:
                    gw.append(f"s_g{pv // 4}_{pv % 4}")
                B.emit("vector", (lambda e, col=col, hs=hs, tb6=tb6: e.tensor_mul(
                    tb6[:], gTok[:, col:col + 512],
                    dscale[:, hs * 512:(hs + 1) * 512])),
                    waits=gw, inc=f"v_g{tc}_{hs}")
                B.emit("scalar", (lambda e, col=col, tb6=tb6: e.activation(
                    gTok[:, col:col + 512], tb6[:], AF.Silu)),
                    waits=[f"v_g{tc}_{hs}"], inc=f"s_g{tc}_{hs}")
            wt = list(wo_ready)
            if tc >= 2:
                wt.append(f"v_o{tc-2}_3")
            B.emit("gp", (lambda e, tc=tc: e.dma_start(
                xres[tc % 2][:], x_d[1024 + tc * 128: 1024 + tc * 128 + 128, 0:1024])),
                waits=wt, inc=f"d_xr{tc}a")
            for hs in range(4):
                bank = hs
                pairs = [(yT[:, q * 1024 + tc * 128: q * 1024 + tc * 128 + 128],
                          wo_bf[:, q * D + hs * 512: q * D + hs * 512 + 512])
                         for q in range(CCN)]
                wf = wo_ready + [f"s_g{tc}_{hs}"]
                mm_group(bank, pairs, ps[bank][:], inc=f"t_s{tc}_{hs}", waits_first=wf)
                n4 = tc * 4 + hs
                pv4 = n4 - 2
                sow = [f"t_s{tc}_{hs}"]
                if pv4 >= 0:
                    sow.append(f"v_o{pv4 // 4}_{pv4 % 4}")
                B.emit("scalar", (lambda e, bank=bank, tb=(t5 if hs % 2 == 0 else t5b):
                       e.activation(tb[:], ps[bank][:], AF.Copy)),
                    waits=sow, inc=f"s_o{tc}_{hs}")
                B.emit("vector", (lambda e, tc=tc, hs=hs,
                                  tb=(t5 if hs % 2 == 0 else t5b): e.tensor_mul(
                    t7[:], tb[:],
                    gTok[:, tc * D + hs * 512: tc * D + hs * 512 + 512])),
                    waits=[f"s_o{tc}_{hs}"])
                if hs == 2:
                    B.emit("gp", (lambda e, tc=tc: e.dma_start(
                        xres[tc % 2][:],
                        x_d[1024 + tc * 128: 1024 + tc * 128 + 128, 1024:2048])),
                        waits=[f"v_o{tc}_1"], inc=f"d_xr{tc}b")
                xw = f"d_xr{tc}a" if hs < 2 else f"d_xr{tc}b"
                ow = [xw]
                if tc >= 2:
                    ow.append(f"d_out{tc-2}")
                B.emit("vector", (lambda e, tc=tc, hs=hs: e.tensor_add(
                    otile[tc % 2][:, hs * 512:(hs + 1) * 512], t7[:],
                    xres[tc % 2][:, (hs % 2) * 512:(hs % 2) * 512 + 512])),
                    waits=ow, inc=f"v_o{tc}_{hs}")
                read_bank(bank, f"v_o{tc}_{hs}")
            B.emit("gp", (lambda e, tc=tc: e.dma_start(
                out_d[tc * 128:(tc + 1) * 128, :], otile[tc % 2][:])),
                waits=[f"v_o{tc}_{hs}" for hs in range(4)], inc=f"d_out{tc}")
            pe_flush()

        local_bufs = dict(locals())
        for nm, bufname, cols, wevt in DEBUG_TAPS:
            buf = local_bufs[bufname]
            dt_ = buf[:].dtype
            tap_d = nc.dram_tensor(f"dbg_{nm}", [128, cols], dt_,
                                   kind="ExternalOutput").ap()
            B.emit("gp", (lambda e, tap_d=tap_d, buf=buf, cols=cols:
                          e.dma_start(tap_d, buf[:, 0:cols])),
                   waits=[wevt], inc=f"d_tap_{nm}")

        B.number()

        rng = mem.ranges
        with nc.Block() as block:
            @block.sync
            def _(sync):
                B.run("sync", sync, sems, rng, drains, collect)
                for j in range(NDS):
                    if f"sync{j}" in B.dma_final:
                        sync.wait_ge(sems[f"sync{j}"], B.dma_final[f"sync{j}"])

            @block.gpsimd
            def _(gp):
                B.run("gpsimd", gp, sems, rng, drains, collect)
                for j in range(NDS):
                    if f"gp{j}" in B.dma_final:
                        gp.wait_ge(sems[f"gp{j}"], B.dma_final[f"gp{j}"])

            @block.tensor
            def _(tensor):
                B.run("tensor", tensor, sems, rng, drains, collect)

            @block.vector
            def _(vector):
                B.run("vector", vector, sems, rng, drains, collect)

            @block.scalar
            def _(scalar):
                B.run("scalar", scalar, sems, rng, drains, collect)

    return nc


_CACHE = {}


def _host_prep(inputs):
    x = np.ascontiguousarray(np.asarray(inputs["L0_attn_hidden_states"]))

    def gu_tiles(w):
        t = np.asarray(w).reshape(KIN, 128, HCN, 128).transpose(0, 3, 2, 1)
        return np.ascontiguousarray(t.reshape(KIN * 128, D).astype(np.int32))

    wgT = gu_tiles(inputs["L0_ffn_gate_w"])
    wuT = gu_tiles(inputs["L0_ffn_up_w"])
    t = np.asarray(inputs["L0_ffn_down_w"]).reshape(4, 512, KIN, 128).transpose(0, 2, 3, 1)
    wdT = np.ascontiguousarray(t.reshape(4 * KIN * 128, 512).astype(np.int32))

    wixT = np.ascontiguousarray(np.asarray(inputs["L0_attn_W_in_x"]).T.astype(np.float32))
    wizT = np.ascontiguousarray(np.asarray(inputs["L0_attn_W_in_z"]).T.astype(np.float32))
    wpT = np.ascontiguousarray(np.asarray(inputs["L0_attn_W_param"]).T.astype(np.float32))
    wdtT = np.ascontiguousarray(np.asarray(inputs["L0_attn_W_dt"]).T.astype(np.float32))
    woT = np.ascontiguousarray(np.asarray(inputs["L0_attn_W_out"]).T.astype(np.float32))

    ebr = np.zeros((NH, CI), np.float32)
    ebr[np.arange(CI) // 16, np.arange(CI)] = 1.0
    ident = np.eye(128, dtype=ml_dtypes.bfloat16)

    ppc = np.zeros((128, 158), np.float32)
    ppc[:, ALOG:ALOG + 4] = np.asarray(inputs["L0_attn_A_log"]).astype(np.float32)\
        .reshape(CI).reshape(CCN, 128).T
    ppc[:, CONVW:CONVW + 16] = np.asarray(inputs["L0_attn_conv_w"]).astype(np.float32)\
        .reshape(CI, 4).reshape(CCN, 128, 4).transpose(1, 0, 2).reshape(128, 16)
    ppc[:, CONVB:CONVB + 4] = np.asarray(inputs["L0_attn_conv_b"]).astype(np.float32)\
        .reshape(CCN, 128).T
    ppc[:, DVEC:DVEC + 4] = np.asarray(inputs["L0_attn_D"]).astype(np.float32)\
        .reshape(CCN, 128).T
    ppc[0:NH, BDT] = np.asarray(inputs["L0_attn_b_dt"]).astype(np.float32)
    ppc[:, GSC:GSC + 64] = np.asarray(inputs["L0_ffn_gate_scale"]).astype(np.float32)\
        .reshape(KIN, 128).T
    ppc[:, USC:USC + 64] = np.asarray(inputs["L0_ffn_up_scale"]).astype(np.float32)\
        .reshape(KIN, 128).T

    dscale = np.ascontiguousarray(np.broadcast_to(
        np.asarray(inputs["L0_ffn_down_scale"]).astype(np.float32)[None, :],
        (128, D)))

    shared = dict(wgT=wgT, wuT=wuT, wdT=wdT, wixT=wixT, wizT=wizT, wpT=wpT,
                  wdtT=wdtT, woT=woT, ebr=ebr, ident=ident, dscale=dscale)

    in_maps = []
    for c in range(8):
        b, half = c // 2, c % 2
        if half == 1:
            xl = x[b]
        else:
            xl = np.concatenate([x[b, 1024:], x[b, :1024]], axis=0)
        ppc_c = ppc.copy()
        ppc_c[:, PMASK] = float(half)
        m = dict(shared)
        m["x"] = np.ascontiguousarray(xl)
        m["ppc"] = ppc_c
        in_maps.append(m)
    return in_maps


def kernel(**inputs):
    if "nc" not in _CACHE:
        _CACHE["nc"] = build_program()
    nc = _CACHE["nc"]
    in_maps = _host_prep(inputs)
    res = run_bass_kernel_spmd(nc, in_maps, core_ids=list(range(8)))
    out = np.empty((4, 2048, 2048), np.float32)
    for c in range(8):
        b, half = c // 2, c % 2
        out[b, half * 1024:(half + 1) * 1024] = res.results[c]["out"]
    return out


# version 09
# speedup vs baseline: 1.0238x; 1.0094x over previous
import numpy as np
import ml_dtypes
from contextlib import ExitStack

import concourse.bass as bass
import concourse.mybir as mybir
from concourse.bass_utils import run_bass_kernel_spmd

F32 = mybir.dt.float32
BF16 = mybir.dt.bfloat16
I32 = mybir.dt.int32
AF = mybir.ActivationFunctionType
ALU = mybir.AluOpType

D = 2048
HCN = 16
CI = 512
CCN = 4
II = 8192
KIN = 64
NQ = 4
LEXT = 2048
LOWN = 1024
BLK = 512
NBLK = 4
DTR = 128
NH = 32

SBUF_BASE = 16448
SBUF_TOP = 229376 - 128

DMA_ENGS = ("sync", "gp")
PHYS = {"sync": "sync", "gp": "gpsimd", "gpc": "gpsimd", "tensor": "tensor",
        "vector": "vector", "scalar": "scalar"}
DMA_SEM_NAMES = {f"{q}{j}" for q in ("sync", "gp") for j in range(64)}

ALOG, CONVW, CONVB, DVEC, BDT, PMASK, GSC, USC = 0, 4, 20, 24, 28, 29, 30, 94


NDS = 8


class Builder:

    def __init__(self):
        self.ops = []
        self.awaited = set()

    def emit(self, eng, fn, waits=(), inc=None):
        ws = [w for w in waits if w is not None]
        self.ops.append(dict(eng=eng, fn=fn, waits=ws, inc=inc))
        for w in ws:
            self.awaited.add(w)

    def number(self):
        self.evt = {}
        cnt = {}
        dma_idx = {}
        for op in self.ops:
            eng, inc = op["eng"], op["inc"]
            op["self_wait"] = None
            if eng in DMA_ENGS:
                i = dma_idx.get(eng, 0)
                dma_idx[eng] = i + 1
                sem = f"{eng}{i % NDS}"
                if i >= NDS:
                    op["self_wait"] = (sem, 16 * (i // NDS))
                assert inc is not None
                self.evt[inc] = (sem, 16 * (i // NDS + 1))
                op["do_inc"] = True
                op["sem"] = sem
            elif inc is not None and inc in self.awaited:
                n = cnt.get(eng, 0) + 1
                cnt[eng] = n
                self.evt[inc] = (eng, n)
                op["do_inc"] = True
                op["sem"] = eng
            else:
                op["do_inc"] = False
            op["phys"] = PHYS[eng]
        for op in self.ops:
            for w in op["waits"]:
                assert w in self.evt, f"wait on unregistered event {w}"
        globals()["LAST_EVT"] = dict(self.evt)
        self.dma_final = {}
        for eng in DMA_ENGS:
            n = dma_idx.get(eng, 0)
            for j in range(NDS):
                k = (n - j + NDS - 1) // NDS
                if k > 0:
                    self.dma_final[f"{eng}{j}"] = 16 * k

    def run(self, phys_name, engine, sems, ranges=None, drains=None,
            collect=None):
        maxw = {}

        def wait(sem, val):
            if maxw.get(sem, -1) >= val:
                return
            maxw[sem] = val
            engine.wait_ge(sems[sem], val)

        idx = 0
        for op in self.ops:
            if op["phys"] != phys_name:
                continue
            for w in op["waits"]:
                sem, val = self.evt[w]
                if sem not in DMA_SEM_NAMES and PHYS.get(sem) == phys_name:
                    continue
                wait(sem, val)
            if op["self_wait"] is not None:
                wait(*op["self_wait"])
            if drains is not None and (phys_name, idx) in drains:
                engine.drain()
            instr = op["fn"](engine)
            if op["do_inc"]:
                instr.then_inc(sems[op["sem"]],
                               16 if op["eng"] in DMA_ENGS else 1)
            if collect is not None and instr is not None:
                collect.setdefault(phys_name, []).append(
                    _rw_ranges(instr.ins, ranges))
            idx += 1


def _rw_ranges(mi, ranges):
    def conv(ap_list):
        out = []
        for x in ap_list:
            memref = getattr(x, "memref", None)
            if memref is None:
                continue
            off = x.offset
            span = 1
            for step, count in list(x.ap)[1:]:
                span += abs(step) * (count - 1)
            sz = mybir.dt.size(x.dtype)
            base = ranges.get(memref)
            if base is not None:
                out.append(("SB", base[0] + off * sz, base[0] + (off + span) * sz))
            else:
                out.append((memref, off * sz, (off + span) * sz))
        return out
    return conv(mi.ins), conv(mi.outs)


def _overlap(a, b):
    for sa, la, ha in a:
        for sb, lb, hb in b:
            if sa == sb and la < hb and lb < ha:
                return True
    return False


def _drain_set(collect):
    drains = set()
    for eng, seq in collect.items():
        if eng in ("sync", "tensor"):
            continue
        for i in range(1, len(seq)):
            r, w = seq[i]
            for j in (i - 1, i - 2):
                if j < 0:
                    break
                pr, pw = seq[j]
                if _overlap(w, pw) or _overlap(r, pw) or _overlap(w, pr):
                    drains.add((eng, i))
                    break
                if (eng, j) in drains:
                    break
    return drains


class Mem:
    def __init__(self, nc):
        self.nc = nc
        self.off = SBUF_BASE
        self.peak = SBUF_BASE
        self.ranges = {}

    def alloc(self, name, shape, dtype, at=None):
        size = shape[1] * mybir.dt.size(dtype)
        if at is None:
            at = self.off
            self.off = at + ((size + 63) // 64) * 64
        self.peak = max(self.peak, at + size)
        assert self.peak <= SBUF_TOP, f"SBUF overflow at {name}: {self.peak}"
        h = self.nc.alloc_sbuf_tensor_at(name, list(shape), dtype, offset=at)
        self.ranges[h.name] = (at, at + size)
        return h

    def reserve(self, size):
        at = self.off
        self.off = at + ((size + 63) // 64) * 64
        self.peak = max(self.peak, self.off)
        assert self.peak <= SBUF_TOP
        return at


DEBUG_TAPS = []
TAP_POINTS = set()
TAPS = {}
BUILD_LIMIT = None
MAMBA_STAGE = None
XP_VARIANT = "full"
N_MM1 = 16
MM1_RHS = "xnT"
MM1_EVAC = True
MM1_PS128 = False
NO_TRANSPOSE = False


def _phase_on(name):
    order = ["startup", "mamba", "gateup", "down", "tail"]
    if BUILD_LIMIT is None:
        return True
    return order.index(name) <= order.index(BUILD_LIMIT)


def _mstage_on(name):
    order = ["ln", "xp", "z", "conv", "params", "scan", "y"]
    if MAMBA_STAGE is None:
        return True
    return order.index(name) <= order.index(MAMBA_STAGE)


def build_program():
    collect = {}
    _build(collect=collect, drains=None)
    drains = _drain_set(collect)
    return _build(collect=None, drains=drains)


def _build(collect, drains):
    nc = bass.Bass("TRN2", target_bir_lowering=False,
                   detect_race_conditions=False)

    x_d = nc.dram_tensor("x", [LEXT, D], F32, kind="ExternalInput").ap()
    wg_d = nc.dram_tensor("wgT", [KIN * 128, D], I32, kind="ExternalInput").ap()
    wu_d = nc.dram_tensor("wuT", [KIN * 128, D], I32, kind="ExternalInput").ap()
    wd_d = nc.dram_tensor("wdT", [4 * KIN * 128, 512], I32, kind="ExternalInput").ap()
    wix_d = nc.dram_tensor("wixT", [D, CI], F32, kind="ExternalInput").ap()
    wiz_d = nc.dram_tensor("wizT", [D, CI], F32, kind="ExternalInput").ap()
    wp_d = nc.dram_tensor("wpT", [CI, 1152], F32, kind="ExternalInput").ap()
    wdt_d = nc.dram_tensor("wdtT", [DTR, NH], F32, kind="ExternalInput").ap()
    wo_d = nc.dram_tensor("woT", [CI, D], F32, kind="ExternalInput").ap()
    ebr_d = nc.dram_tensor("ebr", [NH, CI], F32, kind="ExternalInput").ap()
    id_d = nc.dram_tensor("ident", [128, 128], BF16, kind="ExternalInput").ap()
    ppc_d = nc.dram_tensor("ppc", [128, 158], F32, kind="ExternalInput").ap()
    dsc_d = nc.dram_tensor("dscale", [128, D], F32, kind="ExternalInput").ap()
    out_d = nc.dram_tensor("out", [LOWN, D], F32, kind="ExternalOutput").ap()

    B = Builder()

    with ExitStack() as st:
        mem = Mem(nc)
        al = mem.alloc

        xnT = al("xnT", [128, HCN * 1024], BF16)
        gTok = al("gTok", [128, 8 * D], BF16)
        yT = al("yT", [128, CCN * 1024], BF16)
        ident = al("identS", [128, 128], BF16)
        ebr = al("ebrS", [128, CI], F32)
        ppc = al("ppcS", [128, 158], F32)
        aneg = al("anegS", [128, CCN], F32)
        hcarry = al("hcarryS", [128, CCN], F32)
        wp_bf = al("wp_bf", [128, CCN * 1152], BF16)
        wdt_bf = al("wdt_bf", [128, NH], BF16)
        msum = al("msum", [128, 1], F32)
        ssum = al("ssum", [128, 1], F32)
        meanb = al("meanb", [128, 1], F32)
        msq = al("msq", [128, 1], F32)
        varb = al("varb", [128, 1], F32)
        stdb = al("stdb", [128, 1], F32)
        rstd = al("rstd", [128, 1], F32)
        acc1 = al("acc1", [128, 1], F32)
        acc2 = al("acc2", [128, 1], F32)
        negmr = al("negmr", [128, 1], F32)

        g1 = mem.reserve(32768)
        stage32 = al("stage32", [128, 8192], F32, at=g1)
        hT = al("hT", [128, 16 * 1024], BF16, at=g1)

        g2 = mem.reserve(24576)
        xt = [al(f"xt{i}", [128, D], F32, at=g2 + i * 8192) for i in range(2)]
        xn_tok = al("xn_tok", [128, D], BF16, at=g2 + 16384)
        sqh = al("sqh", [128, 1024], F32, at=g2 + 16384 + 4096)
        i32g = [al(f"i32g{i}", [128, 1024], I32, at=g2 + i * 4096) for i in range(3)]
        i32u = [al(f"i32u{i}", [128, 1024], I32, at=g2 + 12288 + i * 4096) for i in range(3)]
        xres = [al(f"xres{i}", [128, 1024], F32, at=g2 + i * 4096) for i in range(2)]
        otile = [al(f"otile{i}", [128, D], F32, at=g2 + 8192 + i * 8192) for i in range(2)]
        wostg = [al(f"wostg{i}", [128, D], F32, at=g2 + i * 8192) for i in range(2)]

        g3 = mem.reserve(16384)
        xnTblk = al("xnTblk", [128, HCN * 512], BF16, at=g3)
        wiz_bf = al("wiz_bf", [128, HCN * 512], BF16, at=g3)
        bfg = [al(f"bfg{i}", [128, 1024], BF16, at=g3 + i * 2048) for i in range(3)]
        bfu = [al(f"bfu{i}", [128, 1024], BF16, at=g3 + 6144 + i * 2048) for i in range(3)]
        wo_bf = al("wo_bf", [128, CCN * D], BF16, at=g3)

        G4SZ = 32768
        g4 = mem.reserve(G4SZ)
        o = g4
        xp = al("xp", [128, CCN * 520], BF16, at=o); o += 4160
        convacc = al("convacc", [128, 512], F32, at=o); o += 2048
        x_act = al("x_act", [128, CCN * 512], BF16, at=o); o += 4096
        dt_blk = al("dt_blk", [128, 512], BF16, at=o); o += 1024
        b_blk = al("b_blk", [128, 512], BF16, at=o); o += 1024
        c_blk = al("c_blk", [128, 512], BF16, at=o); o += 1024
        spe = al("spe", [128, 512], F32, at=o); o += 2048
        delta = al("delta", [128, 512], F32, at=o); o += 2048
        a_blk = al("a_blk", [128, 512], F32, at=o); o += 2048
        bu_blk = al("bu_blk", [128, 512], F32, at=o); o += 2048
        h_blk = al("h_blk", [128, 512], F32, at=o); o += 2048
        t1 = al("t1", [128, 512], F32, at=o); o += 2048
        t1b = al("t1b", [128, 512], F32, at=o); o += 2048
        siluz = al("siluz", [128, CCN * 512], BF16, at=o); o += 4096
        assert o - g4 <= G4SZ, o - g4
        o = g4
        tsilu = [al(f"tsilu{i}", [128, 512], BF16, at=o + i * 1024) for i in range(4)]
        o += 4096
        i32d = [al(f"i32d{i}", [128, 512], I32, at=o + i * 2048) for i in range(2)]
        o += 4096
        bfd = [al(f"bfd{i}", [128, 512], BF16, at=o + i * 1024) for i in range(2)]
        o += 2048
        t5 = al("t5", [128, 512], F32, at=o); o += 2048
        t6 = al("t6", [128, 512], F32, at=o); o += 2048
        t7 = al("t7", [128, 512], F32, at=o); o += 2048
        t5b = al("t5b", [128, 512], F32, at=o); o += 2048
        t6b = al("t6b", [128, 512], F32, at=o); o += 2048
        assert o - g4 <= G4SZ

        wizstg = [al("wizstg0", [128, 512], F32)] * 2
        g5 = mem.reserve(16384)
        wix_bf = al("wix_bf", [128, HCN * 512], BF16, at=g5)
        dscale = al("dscaleS", [128, D], F32, at=g5)

        ps = [st.enter_context(nc.psum_tensor(f"ps{i}", [128, 512], F32))
              for i in range(8)]
        bank_readers = {i: [] for i in range(8)}

        sems = {}
        for name in ("tensor", "vector", "scalar", "gpc"):
            sems[name] = st.enter_context(nc.semaphore(f"sem_{name}"))
        for q in ("sync", "gp"):
            for j in range(NDS):
                sems[f"{q}{j}"] = st.enter_context(nc.semaphore(f"sem_{q}{j}"))

        pe_pending = []

        def pe_mm(fn, ncols, waits=(), inc=None, inc_direct=None):
            for p in pe_pending:
                p[1] -= ncols
            carry = inc_direct
            if carry is None and pe_pending and pe_pending[0][1] <= 0:
                carry = pe_pending.pop(0)[0]
            B.emit("tensor", fn, waits=waits, inc=carry)
            if inc is not None:
                pe_pending.append([inc, 256])

        def pe_flush():
            while pe_pending:
                evt, _ = pe_pending.pop(0)
                B.emit("tensor", (lambda e: e.drain()), inc=evt)

        def mm_group(bank, pairs, out_ap, inc, waits_first=(), waits_at=None,
                     ncols=512):
            n = len(pairs)
            prev = bank_readers[bank]
            bank_readers[bank] = []
            for i, (l, r) in enumerate(pairs):
                w = list(waits_first) + (prev if i == 0 else [])
                if waits_at and i in waits_at:
                    w = w + waits_at[i]
                pe_mm((lambda e, l=l, r=r, s=(i == 0), p=(i == n - 1), oa=out_ap:
                       e.matmul(oa, l, r, start=s, stop=p)),
                      ncols, waits=w, inc=inc if i == n - 1 else None)

        def read_bank(bank, evt):
            bank_readers[bank].append(evt)

        def tap(eng, label, handle):
            if label not in TAP_POINTS:
                return
            import concourse.bass_interp as BI

            def fn(e, h=handle, label=label):
                def snap(core, h=h, label=label):
                    TAPS[label] = np.asarray(core.tensor(h.name)).copy()
                return BI.add_callback(e, snap)
            B.emit(eng, fn)

        B.emit("sync", lambda e: e.dma_start(ident[:], id_d), inc="d_ident")
        B.emit("sync", lambda e: e.dma_start(ebr[0:NH, :], ebr_d), inc="d_ebr")
        B.emit("sync", lambda e: e.dma_start(ppc[:], ppc_d), inc="d_ppc")
        B.emit("scalar", lambda e: e.activation(aneg[:], ppc[:, ALOG:ALOG + 4], AF.Exp),
               waits=["d_ppc"], inc="s_aneg0")
        B.emit("vector", lambda e: e.tensor_scalar_mul(aneg[:], aneg[:], -1.0),
               waits=["s_aneg0"], inc="v_aneg")
        for c in range(HCN):
            B.emit("sync", (lambda e, c=c: e.dma_start(
                stage32[:, c * 512:(c + 1) * 512], wix_d[c * 128:(c + 1) * 128, :])),
                inc=f"d_wix{c}")
        B.emit("vector", lambda e: e.tensor_copy(wix_bf[:], stage32[:]),
               waits=[f"d_wix{c}" for c in range(HCN - NDS, HCN)], inc="v_cwix")
        for c in range(CCN):
            B.emit("sync", (lambda e, c=c: e.dma_start(
                stage32[:, c * 1152:(c + 1) * 1152], wp_d[c * 128:(c + 1) * 128, :])),
                waits=["v_cwix"] if c == 0 else (), inc=f"d_wp{c}")
        B.emit("vector", lambda e: e.tensor_copy(wp_bf[:], stage32[:, 0:CCN * 1152]),
               waits=[f"d_wp{c}" for c in range(CCN)], inc="v_cwp")
        B.emit("sync", lambda e: e.dma_start(stage32[:, 4608:4640], wdt_d),
               waits=["v_cwp"], inc="d_wdt")
        B.emit("vector", lambda e: e.tensor_copy(wdt_bf[:], stage32[:, 4608:4640]),
               waits=["d_wdt"], inc="v_cwdt")

        gtok_at = mem.ranges[gTok.name][0]
        i32e = [al(f"i32e{i}", [128, 1024], I32, at=gtok_at + i * 4096)
                for i in range(3)]
        bfge = [al(f"bfge{i}", [128, 1024], BF16, at=gtok_at + 12288 + i * 2048)
                for i in range(4)]
        bfue = [al(f"bfue{i}", [128, 1024], BF16, at=gtok_at + 20480 + i * 2048)
                for i in range(4)]
        tsie = [al(f"tsie{i}", [128, 512], BF16, at=gtok_at + 28672 + i * 1024)
                for i in range(4)]

        zmm_last = None
        ln_done_evts = []
        blk_trdone = {}
        hst_evts = {}
        ffn_k = [0]
        load_k = [0]
        feed_cap = [99]
        ln_order = [8, 9, 10, 11, 12, 13, 14, 15, 0, 1, 2, 3, 4, 5, 6, 7]
        ln_pos = {g: s for s, g in enumerate(ln_order)}

        def feed_load(k):
            if k >= 16:
                return
            for wi, which in enumerate(("g", "u")):
                src_d = wg_d if which == "g" else wu_d
                bfl = bfge if which == "g" else bfue
                for half in range(2):
                    gi = 2 * k + half
                    ti = 4 * k + wi * 2 + half
                    islot = ti % 3
                    wt = []
                    if ti >= 3:
                        pt = ti - 3
                        pw = "g" if (pt % 4) < 2 else "u"
                        pgi = 2 * (pt // 4) + (pt % 2)
                        wt.append(f"s_c{pw}{pgi}")
                    B.emit("sync", (lambda e, src_d=src_d, dst=i32e[islot], k=k,
                                    lo=half * 1024: e.dma_start(
                        dst[:], src_d[k * 128:(k + 1) * 128, lo:lo + 1024])),
                        waits=wt, inc=f"d_{which}{gi}")
                    cwt = [f"d_{which}{gi}"]
                    if gi >= 4:
                        cwt.append(f"t_gu{(gi-4)//2}_{which}1")
                    B.emit("scalar", (lambda e, bfs=bfl[gi % 4],
                                      i32s=i32e[islot]: e.activation(
                        bfs[:], i32s[:], AF.Copy, bias=-128.0)),
                        waits=cwt, inc=f"s_c{which}{gi}")

        def ffn_feed(nk):
            if not _phase_on("gateup"):
                return
            for _ in range(nk):
                if feed_cap[0] <= 0:
                    return
                feed_cap[0] -= 1
                k = ffn_k[0]
                if k >= 16:
                    return
                ffn_k[0] = k + 1
                pe_flush()
                while load_k[0] <= min(k + 1, 15):
                    feed_load(load_k[0])
                    load_k[0] += 1
                for which in ("g", "u"):
                    wslots = bfge if which == "g" else bfue
                    for tch in range(2):
                        bank = 4 + (0 if which == "g" else 2) + tch
                        pairs = []
                        for c in range(HCN):
                            half = c // 8
                            sl = wslots[(2 * k + half) % 4]
                            pairs.append((sl[:, (c % 8) * 128:(c % 8 + 1) * 128],
                                          xnT[:, c * 1024 + tch * 512: c * 1024 + tch * 512 + 512]))
                        wa = {0: [f"s_c{which}{2*k}"], 8: [f"s_c{which}{2*k+1}"]}
                        if k == 0:
                            wa[0] = wa[0] + blk_trdone[2] + blk_trdone[3]
                        mm_group(bank, pairs, ps[bank][:],
                                 inc=f"t_gu{k}_{which}{tch}", waits_at=wa)
                for tch in range(2):
                    gbank = 4 + tch
                    ubank = 6 + tch
                    tslot = (k % 2) * 2 + tch
                    sw = [f"t_gu{k}_g{tch}"]
                    if k >= 2:
                        sw.append(f"v_h{k-2}_{tch}")
                    B.emit("scalar", (lambda e, k=k, tslot=tslot, gbank=gbank:
                           e.activation(
                        tsie[tslot][:], ps[gbank][:], AF.Silu,
                        scale=ppc[:, GSC + k:GSC + k + 1])),
                        waits=sw, inc=f"s_sil{k}_{tch}")
                    read_bank(gbank, f"s_sil{k}_{tch}")
                    hw = [f"t_gu{k}_u{tch}", f"s_sil{k}_{tch}"]
                    if k == 0:
                        hw.append("v_cwdt")
                    B.emit("vector", (lambda e, k=k, tch=tch, tslot=tslot,
                                      ubank=ubank: e.scalar_tensor_tensor(
                        hT[:, (k % 16) * 1024 + tch * 512:(k % 16) * 1024 + tch * 512 + 512],
                        ps[ubank][:], ppc[:, USC + k:USC + k + 1],
                        tsie[tslot][:], ALU.mult, ALU.mult)),
                        waits=hw, inc=f"v_h{k}_{tch}")
                    read_bank(ubank, f"v_h{k}_{tch}")
                    hst_evts[(k, tch)] = f"v_h{k}_{tch}"
                pe_flush()

        def emit_ln_tile(g, fw):
            s = ln_pos[g]
            b, j = g // 4, g % 4
            p1 = ln_order[s - 1] if s >= 1 else None
            p2 = ln_order[s - 2] if s >= 2 else None
            slot = s % 2
            if b >= 2:
                def dst(c, b=b, j=j):
                    lo = c * 1024 + (b - 2) * 512 + j * 128
                    return xnT[:, lo:lo + 128]
            else:
                def dst(c, j=j):
                    return xnTblk[:, c * 512 + j * 128: c * 512 + j * 128 + 128]
            wt = []
            if p2 is not None:
                wt = [f"s_norm{p2}", f"v_red{p2}"]
            B.emit("gp", (lambda e, g=g, slot=slot: e.dma_start(
                xt[slot][:], x_d[g * 128:(g + 1) * 128, :])),
                waits=wt, inc=f"d_x{g}")
            sq_w = [f"d_x{g}"] + fw("scalar")
            if p1 is not None:
                sq_w += [f"v_red{p1}"]
            B.emit("scalar", (lambda e, slot=slot: e.activation(
                sqh[:], xt[slot][:, 0:1024], AF.Square, accum_out=acc1[:])),
                waits=sq_w, inc=None)
            B.emit("scalar", (lambda e, slot=slot: e.activation(
                sqh[:], xt[slot][:, 1024:2048], AF.Square, accum_out=acc2[:])),
                inc=f"s_sq{g}")
            B.emit("vector", (lambda e, slot=slot: e.reduce_sum(
                msum[:], xt[slot][:], axis=mybir.AxisListType.X)),
                waits=[f"d_x{g}"] + fw("vector"))
            B.emit("vector", (lambda e: e.tensor_add(
                ssum[:], acc1[:], acc2[:])),
                waits=[f"s_sq{g}"], inc=f"v_red{g}")
            B.emit("vector", lambda e: e.tensor_scalar_mul(meanb[:], msum[:], 1.0 / D),
                   waits=[f"s_norm{p1}"] if p1 is not None else [])
            B.emit("vector", lambda e: e.tensor_scalar(
                msq[:], meanb[:], meanb[:], 1e-5, ALU.mult, ALU.subtract))
            B.emit("vector", lambda e: e.tensor_scalar(
                varb[:], ssum[:], 1.0 / D, msq[:], ALU.mult, ALU.subtract))
            B.emit("vector", lambda e: e.tensor_scalar_mul(negmr[:], meanb[:], -1.0),
                   inc=f"v_lnt{g}")
            B.emit("scalar", lambda e: e.activation(stdb[:], varb[:], AF.Sqrt),
                   waits=[f"v_lnt{g}"], inc=f"s_std{g}")
            B.emit("vector", lambda e: e.reciprocal(rstd[:], stdb[:]),
                   waits=[f"s_std{g}"])
            B.emit("vector", (lambda e, slot=slot: e.tensor_scalar(
                xn_tok[:], xt[slot][:], negmr[:], rstd[:], ALU.add, ALU.mult)),
                waits=[f"t_tr{p1}_{HCN-1}"] if p1 is not None else [],
                inc=f"s_norm{g}")
            for c in range(HCN):
                bank = c % 4
                prev = bank_readers[bank]
                bank_readers[bank] = []
                pe_mm((lambda e, c=c, bank=bank: e.matmul(
                    ps[bank][:, 0:128], xn_tok[:, c * 128:(c + 1) * 128],
                    ident[:], start=True, stop=True)), 128,
                    waits=[f"s_norm{g}", "d_ident"] + prev + fw("tensor"),
                    inc=f"t_tr{g}_{c}")
                B.emit("vector", (lambda e, c=c, bank=bank, dst=dst:
                       e.tensor_copy(dst(c), ps[bank][:, 0:128])),
                       waits=[f"t_tr{g}_{c}"], inc=f"v_tr{g}_{c}")
                read_bank(bank, f"v_tr{g}_{c}")
            pe_flush()

        for si, g in enumerate(ln_order[:8]):
            emit_ln_tile(g, lambda eng: [])
            if si == 5 and _phase_on("gateup"):
                feed_load(0)
                load_k[0] = 1
        for b in (2, 3):
            blk_trdone[b] = [f"v_tr{b*4+j}_{c}" for j in range(4)
                             for c in range(HCN)]
        ffn_feed(3)

        for b in range(NBLK):
            own = b >= 2
            if own:
                def xcols(c, b=b):
                    lo = c * 1024 + (b - 2) * 512
                    return xnT[:, lo:lo + 512]
            else:
                def xcols(c):
                    return xnTblk[:, c * 512:(c + 1) * 512]

            if b >= 1:
                first_w = {"scalar": [f"bar_v{b-1}", f"bar_t{b-1}"],
                           "vector": [f"bar_s{b-1}", f"bar_t{b-1}"],
                           "tensor": [f"bar_s{b-1}", f"bar_v{b-1}"]}
            else:
                first_w = {"scalar": [], "vector": [], "tensor": []}
            fw_used = {k: False for k in first_w}

            def fw(eng):
                if fw_used[eng]:
                    return []
                fw_used[eng] = True
                return first_w[eng]

            feed_cap[0] = {0: 3, 1: 4, 2: 5, 3: 8}[b]
            if not own:
                for j in range(4):
                    emit_ln_tile(b * 4 + j, fw)
                blk_trdone[b] = [f"v_tr{b*4+j}_{c}" for j in range(4)
                                 for c in range(HCN)]
                if b == 1:
                    g = 7
                    ln_done_evts += [f"s_norm{g}", f"v_red{g}",
                                     f"t_tr{g}_{HCN-1}"]
            trdone = blk_trdone[b]
            ffn_feed(1)

            for cc in range(CCN if (_mstage_on("xp") and XP_VARIANT == "full") else 0):
                last = (cc == CCN - 1 and b != 2)
                if b == 0:
                    B.emit("vector", (lambda e, cc=cc: e.memset(
                        xp[:, cc * 520:cc * 520 + 3], 0.0)),
                        inc="v_carry0" if last else None)
                else:
                    B.emit("vector", (lambda e, cc=cc: e.tensor_copy(
                        xp[:, cc * 520:cc * 520 + 3],
                        xp[:, cc * 520 + 512:cc * 520 + 515])),
                        waits=fw("vector"),
                        inc=f"v_carry{b}" if last else None)
                    if b == 2:
                        B.emit("vector", (lambda e, cc=cc: e.tensor_scalar_mul(
                            xp[:, cc * 520:cc * 520 + 3],
                            xp[:, cc * 520:cc * 520 + 3],
                            ppc[:, PMASK:PMASK + 1])),
                            inc=f"v_carry{b}" if cc == CCN - 1 else None)

            for cc in range(CCN if _mstage_on("xp") else 0):
                bank = 0 + (cc % 2)
                pairs = [(wix_bf[:, c * 512 + cc * 128: c * 512 + (cc + 1) * 128],
                          xcols(c)) for c in range(HCN)]
                mm_group(bank, pairs, ps[bank][:], inc=f"t_xp{b}_{cc}",
                         waits_first=(["v_cwix"] + trdone + fw("tensor"))
                         if cc == 0 else [])
                B.emit("vector", (lambda e, cc=cc, bank=bank: e.tensor_copy(
                    xp[:, cc * 520 + 3:cc * 520 + 515], ps[bank][:])),
                    waits=[f"t_xp{b}_{cc}", f"v_carry{b}"], inc=f"v_xp{b}_{cc}")
                read_bank(bank, f"v_xp{b}_{cc}")
            if _mstage_on("xp"):
                pe_flush()
            if b == 1 and _mstage_on("z"):
                for c in range(HCN):
                    wwt = [f"t_xp1_{CCN-1}"] if c < 1 else [f"v_wz{c-1}"]
                    B.emit("sync", (lambda e, c=c: e.dma_start(
                        wizstg[c % 2][:], wiz_d[c * 128:(c + 1) * 128, :])),
                        waits=wwt, inc=f"d_wiz{c}")
                    B.emit("vector", (lambda e, c=c: e.tensor_copy(
                        wiz_bf[:, c * 512:(c + 1) * 512], wizstg[c % 2][:])),
                        waits=[f"d_wiz{c}"], inc=f"v_wz{c}")
                B.emit("vector", lambda e: e.nop(), inc="v_cwiz")
            ffn_feed(1)

            if own and _mstage_on("z"):
                for cc in range(CCN):
                    bank = 2 + (cc % 2)
                    pairs = [(wiz_bf[:, c * 512 + cc * 128: c * 512 + (cc + 1) * 128],
                              xcols(c)) for c in range(HCN)]
                    mm_group(bank, pairs, ps[bank][:], inc=f"t_z{b}_{cc}",
                             waits_first=["v_cwiz"] if (b == 2 and cc == 0) else [])
                    B.emit("scalar", (lambda e, cc=cc, bank=bank: e.activation(
                        siluz[:, cc * 512:(cc + 1) * 512], ps[bank][:], AF.Silu)),
                        waits=[f"t_z{b}_{cc}"] + fw("scalar"), inc=f"s_z{b}_{cc}")
                    read_bank(bank, f"s_z{b}_{cc}")
                    zmm_last = f"t_z{b}_{cc}"
                pe_flush()

            for cc in range(CCN if _mstage_on("conv") else 0):
                base = cc * 520
                B.emit("vector", (lambda e, base=base, cc=cc: e.tensor_scalar_mul(
                    convacc[:], xp[:, base:base + 512],
                    ppc[:, CONVW + cc * 4:CONVW + cc * 4 + 1])),
                    waits=[f"v_xp{b}_{cc}"]
                    + ([f"s_xact{b}_{cc-1}"] if cc >= 1 else []))
                for k in range(1, 4):
                    B.emit("vector", (lambda e, base=base, cc=cc, k=k:
                           e.scalar_tensor_tensor(
                               convacc[:], xp[:, base + k:base + 512 + k],
                               ppc[:, CONVW + cc * 4 + k:CONVW + cc * 4 + k + 1],
                               convacc[:], ALU.mult, ALU.add)),
                           inc=f"v_conv{b}_{cc}" if k == 3 else None)
                B.emit("scalar", (lambda e, cc=cc: e.activation(
                    x_act[:, cc * 512:(cc + 1) * 512], convacc[:], AF.Silu,
                    bias=ppc[:, CONVB + cc:CONVB + cc + 1])),
                    waits=[f"v_conv{b}_{cc}"], inc=f"s_xact{b}_{cc}")
                tap("scalar", f"xact{b}_{cc}", x_act)
                if cc == 1 or cc == 3:
                    ffn_feed(1)

            xact_w = [f"s_xact{b}_{cc}" for cc in range(CCN)]

            if not _mstage_on("params"):
                pe_flush()
                B.emit("tensor", lambda e: e.nop(), inc=f"bar_t{b}")
                B.emit("vector", lambda e: e.nop(), inc=f"bar_v{b}")
                B.emit("scalar", lambda e: e.nop(), inc=f"bar_s{b}")
                continue
            bank = 0
            pairs = [(wp_bf[:, cc * 1152: cc * 1152 + 128],
                      x_act[:, cc * 512:(cc + 1) * 512]) for cc in range(CCN)]
            mm_group(bank, pairs, ps[bank][:], inc=f"t_dt{b}",
                     waits_first=xact_w + ["v_cwp"])
            pe_flush()
            B.emit("vector", (lambda e: e.tensor_copy(dt_blk[:], ps[0][:])),
                   waits=[f"t_dt{b}"], inc=f"v_dt{b}")
            tap("vector", f"dt{b}", dt_blk)
            read_bank(0, f"v_dt{b}")
            mm_group(1, [(wdt_bf[:], dt_blk[:])], ps[1][0:NH, :],
                     inc=f"t_del{b}", waits_first=[f"v_dt{b}", "v_cwdt"])
            pe_flush()
            B.emit("scalar", (lambda e: e.activation(
                spe[0:NH, :], ps[1][0:NH, :], AF.Exp, bias=ppc[0:NH, BDT:BDT + 1])),
                waits=[f"t_del{b}"], inc=f"s_spe{b}")
            read_bank(1, f"s_spe{b}")
            B.emit("scalar", lambda e: e.activation(
                delta[0:NH, :], spe[0:NH, :], AF.Ln, bias=1.0), inc=f"s_del{b}")
            tap("scalar", f"delta{b}", delta)
            ffn_feed(1)

            for q in range(CCN if _mstage_on("scan") else 0):
                bank = 0 + (q % 2)
                pairs = [(wp_bf[:, cc * 1152 + (1 + q) * 128: cc * 1152 + (2 + q) * 128],
                          x_act[:, cc * 512:(cc + 1) * 512]) for cc in range(CCN)]
                mm_group(bank, pairs, ps[bank][:], inc=f"t_pb{b}_{q}")
                B.emit("vector", (lambda e, bank=bank: e.tensor_copy(
                    b_blk[:], ps[bank][:])),
                    waits=[f"t_pb{b}_{q}"], inc=f"v_pb{b}_{q}")
                tap("vector", f"b{b}_{q}", b_blk)
                read_bank(bank, f"v_pb{b}_{q}")
                if own:
                    bank = 2 + (q % 2)
                    pairs = [(wp_bf[:, cc * 1152 + (5 + q) * 128: cc * 1152 + (6 + q) * 128],
                              x_act[:, cc * 512:(cc + 1) * 512]) for cc in range(CCN)]
                    mm_group(bank, pairs, ps[bank][:], inc=f"t_pc{b}_{q}")
                    B.emit("vector", (lambda e, bank=bank: e.tensor_copy(
                        c_blk[:], ps[bank][:])),
                        waits=[f"t_pc{b}_{q}"], inc=f"v_pc{b}_{q}")
                    read_bank(bank, f"v_pc{b}_{q}")
                bcb = (0 if own else 2) + (q % 2)
                mm_group(bcb, [(ebr[0:NH, q * 128:(q + 1) * 128], delta[0:NH, :])],
                         ps[bcb][:], inc=f"t_bc{b}_{q}",
                         waits_first=[f"s_del{b}", "d_ebr"])
                scal = 1.0 if own else ppc[:, PMASK:PMASK + 1]
                B.emit("vector", (lambda e, bcb=bcb, scal=scal: e.scalar_tensor_tensor(
                    t1[:], ps[bcb][:], scal, b_blk[:], ALU.mult, ALU.mult)),
                    waits=[f"t_bc{b}_{q}", f"v_pb{b}_{q}"], inc=f"v_t1{b}_{q}")
                B.emit("vector", (lambda e, q=q: e.tensor_mul(
                    bu_blk[:], t1[:], x_act[:, q * 512:(q + 1) * 512])),
                    inc=f"v_bu{b}_{q}")
                tap("vector", f"bu{b}_{q}", bu_blk)
                tap("vector", f"t1_{b}_{q}", t1)
                aw = [f"v_t1{b}_{q}", "v_aneg"]
                if q > 0:
                    aw.append(f"v_scan{b}_{q-1}")
                elif b > 0:
                    aw.append(f"bar_v{b-1}")
                B.emit("scalar", (lambda e, q=q, bcb=bcb: e.activation(
                    a_blk[:], ps[bcb][:], AF.Exp, scale=aneg[:, q:q + 1])),
                    waits=aw, inc=f"s_a{b}_{q}")
                read_bank(bcb, f"s_a{b}_{q}")
                init = 0.0 if b == 0 else hcarry[:, q:q + 1]
                B.emit("vector", (lambda e, init=init: e.tensor_tensor_scan(
                    h_blk[:], a_blk[:], bu_blk[:], init, ALU.mult, ALU.add)),
                    waits=[f"s_a{b}_{q}"], inc=f"v_scan{b}_{q}")
                B.emit("vector", (lambda e, q=q: e.tensor_copy(
                    hcarry[:, q:q + 1], h_blk[:, 511:512])))
                ffn_feed(1)
                if own:
                    B.emit("vector", (lambda e: e.tensor_mul(
                        t1[:], h_blk[:], c_blk[:])),
                        waits=[f"v_pc{b}_{q}"])
                    B.emit("vector", (lambda e, q=q: e.scalar_tensor_tensor(
                        t1b[:], x_act[:, q * 512:(q + 1) * 512],
                        ppc[:, DVEC + q:DVEC + q + 1], t1[:], ALU.mult, ALU.add)))
                    B.emit("vector", (lambda e, q=q, b=b: e.tensor_mul(
                        yT[:, q * 1024 + (b - 2) * 512: q * 1024 + (b - 2) * 512 + 512],
                        t1b[:], siluz[:, q * 512:(q + 1) * 512])),
                        waits=[f"s_z{b}_{q}"], inc=f"v_y{b}_{q}")

            pe_flush()
            B.emit("tensor", lambda e: e.nop(), inc=f"bar_t{b}")
            B.emit("vector", lambda e: e.nop(), inc=f"bar_v{b}")
            B.emit("scalar", lambda e: e.nop(), inc=f"bar_s{b}")

        mamba_done = [f"bar_t{NBLK-1}", f"bar_v{NBLK-1}", f"bar_s{NBLK-1}"]
        feed_cap[0] = 99
        ffn_feed(16)

        B.emit("sync", lambda e: e.dma_start(dscale[:], dsc_d),
               waits=[f"t_xp{NBLK-1}_{CCN-1}"]
               if (_mstage_on("xp") and XP_VARIANT not in ("mm1",)) else [],
               inc="d_dsc")

        down_done = []
        for q4 in range(NQ if _phase_on("gateup") else 0):
            for kk in range(16 if q4 >= 1 else 0):
                k = q4 * 16 + kk
                for which in ("g", "u"):
                    src = wg_d if which == "g" else wu_d
                    i32s_l = i32g if which == "g" else i32u
                    bfs_l = bfg if which == "g" else bfu
                    for half in range(2):
                        gi = 2 * k + half
                        slot = gi % 3
                        wt = []
                        if k == 16 and half == 0 and which == "g":
                            wt = list(ln_done_evts)
                        if gi >= 3:
                            wt.append(f"s_c{which}{gi-3}")
                        B.emit("sync", (lambda e, src=src, dst=i32s_l[slot], k=k,
                                        lo=half * 1024: e.dma_start(
                            dst[:], src[k * 128:(k + 1) * 128, lo:lo + 1024])),
                            waits=wt, inc=f"d_{which}{gi}")
                        cwt = [f"d_{which}{gi}"]
                        if k == 16 and half == 0:
                            cwt.append(zmm_last)
                        if gi >= 3:
                            cwt.append(f"t_gu{(gi-3)//2}_{which}1")
                        B.emit("scalar", (lambda e, bfs=bfs_l[slot], i32s=i32s_l[slot]:
                               e.activation(bfs[:], i32s[:], AF.Copy, bias=-128.0)),
                            waits=cwt, inc=f"s_c{which}{gi}")
                for which in ("g", "u"):
                    wslots = bfg if which == "g" else bfu
                    for tch in range(2):
                        bank = (k % 2) * 4 + (0 if which == "g" else 2) + tch
                        pairs = []
                        for c in range(HCN):
                            half = c // 8
                            sl = wslots[(2 * k + half) % 3]
                            pairs.append((sl[:, (c % 8) * 128:(c % 8 + 1) * 128],
                                          xnT[:, c * 1024 + tch * 512: c * 1024 + tch * 512 + 512]))
                        wa = {0: [f"s_c{which}{2*k}"] + (mamba_done if k == 16 else []),
                              8: [f"s_c{which}{2*k+1}"]}
                        mm_group(bank, pairs, ps[bank][:],
                                 inc=f"t_gu{k}_{which}{tch}", waits_at=wa)
                for tch in range(2):
                    gbank = (k % 2) * 4 + tch
                    ubank = (k % 2) * 4 + 2 + tch
                    tslot = (k % 2) * 2 + tch
                    sw = [f"t_gu{k}_g{tch}"]
                    if q4 > 0 and kk == 0 and down_done:
                        sw.append(down_done[-1])
                    if k >= 2:
                        sw.append(f"v_h{k-2}_{tch}")
                    B.emit("scalar", (lambda e, k=k, tslot=tslot, gbank=gbank:
                           e.activation(
                        tsilu[tslot][:], ps[gbank][:], AF.Silu,
                        scale=ppc[:, GSC + k:GSC + k + 1])),
                        waits=sw, inc=f"s_sil{k}_{tch}")
                    read_bank(gbank, f"s_sil{k}_{tch}")
                    hw = [f"t_gu{k}_u{tch}", f"s_sil{k}_{tch}"]
                    if q4 > 0 and down_done:
                        hw.append(down_done[-1])
                    B.emit("vector", (lambda e, k=k, tch=tch, tslot=tslot, ubank=ubank:
                           e.scalar_tensor_tensor(
                               hT[:, (k % 16) * 1024 + tch * 512:(k % 16) * 1024 + tch * 512 + 512],
                               ps[ubank][:], ppc[:, USC + k:USC + k + 1],
                               tsilu[tslot][:], ALU.mult, ALU.mult)),
                           waits=hw, inc=f"v_h{k}_{tch}")
                    read_bank(ubank, f"v_h{k}_{tch}")
                    hst_evts[(k, tch)] = f"v_h{k}_{tch}"

            pe_flush()
            if q4 == 3 and _phase_on("tail"):
                for c in range(CCN):
                    B.emit("sync", (lambda e, c=c: e.dma_start(
                        wostg[c % 2][:], wo_d[c * 128:(c + 1) * 128, :])),
                        waits=[f"s_cg{2*KIN-1}", f"s_cu{2*KIN-1}"]
                        + ([f"v_cwo{c-2}"] if c >= 2 else []),
                        inc=f"d_wo{c}")
                    B.emit("vector", (lambda e, c=c: e.tensor_copy(
                        wo_bf[:, c * D:(c + 1) * D], wostg[c % 2][:])),
                        waits=[f"d_wo{c}", f"t_gu{KIN-1}_u1"],
                        inc=f"v_cwo{c}")
            if not _phase_on("down"):
                continue
            hT_ready = [hst_evts[(q4 * 16 + kk, tch)] for kk in range(16) for tch in range(2)]
            for hs in range(4):
                for kk in range(16):
                    gidx = (q4 * 4 + hs) * 16 + kk
                    row = (hs * KIN + q4 * 16 + kk) * 128
                    wt = []
                    if gidx < 2:
                        wt = list(mamba_done)
                    else:
                        wt.append(f"s_cd{gidx-2}")
                    B.emit("sync", (lambda e, row=row, sl=gidx % 2: e.dma_start(
                        i32d[sl][:], wd_d[row:row + 128, :])),
                        waits=wt, inc=f"d_wd{gidx}")
                    cwt = [f"d_wd{gidx}"]
                    if gidx < 2:
                        cwt += mamba_done
                    else:
                        p = gidx - 2
                        pq, prem = divmod(p, 64)
                        phs, pkk = divmod(prem, 16)
                        cwt.append(f"t_dn{pq}_{phs}_{pkk}")
                    B.emit("scalar", (lambda e, sl=gidx % 2: e.activation(
                        bfd[sl][:], i32d[sl][:], AF.Copy, bias=-128.0)),
                        waits=cwt, inc=f"s_cd{gidx}")
                    for tc in range(8):
                        prev = bank_readers[tc] if kk == 0 else []
                        if kk == 0:
                            bank_readers[tc] = []
                        w = ([f"s_cd{gidx}"] if tc == 0 else []) + prev
                        if kk == 0 and tc == 0 and hs == 0:
                            w = w + hT_ready
                        pe_mm((lambda e, kk=kk, tc=tc, sl=gidx % 2:
                               e.matmul(ps[tc][:],
                                        hT[:, kk * 1024 + tc * 128: kk * 1024 + tc * 128 + 128],
                                        bfd[sl][:],
                                        start=(kk == 0), stop=(kk == 15))), 512,
                              waits=w,
                              inc_direct=(f"t_dn{q4}_{hs}_{kk}"
                                          if (tc == 7 and kk < 15) else None),
                              inc=(f"t_dn{q4}_{hs}_{kk}"
                                   if (tc == 7 and kk == 15) else None))
                pe_flush()
                for tc in range(8):
                    col = tc * D + hs * 512
                    if q4 == 0:
                        gw = [f"t_dn{q4}_{hs}_15"]
                        if hs == 0 and tc == 0:
                            gw += [f"s_cg{31}", f"s_cu{31}", f"v_h{15}_{1}"]
                        B.emit("vector", (lambda e, col=col, tc=tc: e.tensor_copy(
                            gTok[:, col:col + 512], ps[tc][:])),
                            waits=gw, inc=f"v_dn{q4}_{hs}_{tc}")
                    elif q4 < 3:
                        B.emit("vector", (lambda e, col=col, tc=tc: e.tensor_add(
                            gTok[:, col:col + 512], ps[tc][:], gTok[:, col:col + 512])),
                            waits=[f"t_dn{q4}_{hs}_15"], inc=f"v_dn{q4}_{hs}_{tc}")
                    else:
                        B.emit("vector", (lambda e, col=col, tc=tc: e.tensor_add(
                            gTok[:, col:col + 512], ps[tc][:],
                            gTok[:, col:col + 512])),
                            waits=[f"t_dn{q4}_{hs}_15"], inc=f"v_dn{q4}_{hs}_{tc}")
                    read_bank(tc, f"v_dn{q4}_{hs}_{tc}")
            down_done.append(f"t_dn{q4}_3_15")

        wo_ready = [f"v_cwo{c}" for c in range(CCN)]
        for tc in range(8):
            for hs in range(4):
                col = tc * D + hs * 512
                n8 = tc * 4 + hs
                tb6 = t6 if n8 % 2 == 0 else t6b
                pv = n8 - 2
                gw = [f"v_dn{3}_{hs}_{tc}", "d_dsc"]
                if pv >= 0:
                    gw.append(f"s_g{pv // 4}_{pv % 4}")
                B.emit("vector", (lambda e, col=col, hs=hs, tb6=tb6: e.tensor_mul(
                    tb6[:], gTok[:, col:col + 512],
                    dscale[:, hs * 512:(hs + 1) * 512])),
                    waits=gw, inc=f"v_g{tc}_{hs}")
                B.emit("scalar", (lambda e, col=col, tb6=tb6: e.activation(
                    gTok[:, col:col + 512], tb6[:], AF.Silu)),
                    waits=[f"v_g{tc}_{hs}"], inc=f"s_g{tc}_{hs}")
            wt = list(wo_ready)
            if tc >= 2:
                wt.append(f"v_o{tc-2}_3")
            B.emit("gp", (lambda e, tc=tc: e.dma_start(
                xres[tc % 2][:], x_d[1024 + tc * 128: 1024 + tc * 128 + 128, 0:1024])),
                waits=wt, inc=f"d_xr{tc}a")
            for hs in range(4):
                bank = hs
                pairs = [(yT[:, q * 1024 + tc * 128: q * 1024 + tc * 128 + 128],
                          wo_bf[:, q * D + hs * 512: q * D + hs * 512 + 512])
                         for q in range(CCN)]
                wf = wo_ready + [f"s_g{tc}_{hs}"]
                mm_group(bank, pairs, ps[bank][:], inc=f"t_s{tc}_{hs}", waits_first=wf)
                n4 = tc * 4 + hs
                pv4 = n4 - 2
                sow = [f"t_s{tc}_{hs}"]
                if pv4 >= 0:
                    sow.append(f"v_o{pv4 // 4}_{pv4 % 4}")
                B.emit("scalar", (lambda e, bank=bank, tb=(t5 if hs % 2 == 0 else t5b):
                       e.activation(tb[:], ps[bank][:], AF.Copy)),
                    waits=sow, inc=f"s_o{tc}_{hs}")
                B.emit("vector", (lambda e, tc=tc, hs=hs,
                                  tb=(t5 if hs % 2 == 0 else t5b): e.tensor_mul(
                    t7[:], tb[:],
                    gTok[:, tc * D + hs * 512: tc * D + hs * 512 + 512])),
                    waits=[f"s_o{tc}_{hs}"])
                if hs == 2:
                    B.emit("gp", (lambda e, tc=tc: e.dma_start(
                        xres[tc % 2][:],
                        x_d[1024 + tc * 128: 1024 + tc * 128 + 128, 1024:2048])),
                        waits=[f"v_o{tc}_1"], inc=f"d_xr{tc}b")
                xw = f"d_xr{tc}a" if hs < 2 else f"d_xr{tc}b"
                ow = [xw]
                if tc >= 2:
                    ow.append(f"d_out{tc-2}")
                B.emit("vector", (lambda e, tc=tc, hs=hs: e.tensor_add(
                    otile[tc % 2][:, hs * 512:(hs + 1) * 512], t7[:],
                    xres[tc % 2][:, (hs % 2) * 512:(hs % 2) * 512 + 512])),
                    waits=ow, inc=f"v_o{tc}_{hs}")
                read_bank(bank, f"v_o{tc}_{hs}")
            B.emit("gp", (lambda e, tc=tc: e.dma_start(
                out_d[tc * 128:(tc + 1) * 128, :], otile[tc % 2][:])),
                waits=[f"v_o{tc}_{hs}" for hs in range(4)], inc=f"d_out{tc}")
            pe_flush()

        local_bufs = dict(locals())
        for nm, bufname, cols, wevt in DEBUG_TAPS:
            buf = local_bufs[bufname]
            dt_ = buf[:].dtype
            tap_d = nc.dram_tensor(f"dbg_{nm}", [128, cols], dt_,
                                   kind="ExternalOutput").ap()
            B.emit("gp", (lambda e, tap_d=tap_d, buf=buf, cols=cols:
                          e.dma_start(tap_d, buf[:, 0:cols])),
                   waits=[wevt], inc=f"d_tap_{nm}")

        B.number()

        rng = mem.ranges
        with nc.Block() as block:
            @block.sync
            def _(sync):
                B.run("sync", sync, sems, rng, drains, collect)
                for j in range(NDS):
                    if f"sync{j}" in B.dma_final:
                        sync.wait_ge(sems[f"sync{j}"], B.dma_final[f"sync{j}"])

            @block.gpsimd
            def _(gp):
                B.run("gpsimd", gp, sems, rng, drains, collect)
                for j in range(NDS):
                    if f"gp{j}" in B.dma_final:
                        gp.wait_ge(sems[f"gp{j}"], B.dma_final[f"gp{j}"])

            @block.tensor
            def _(tensor):
                B.run("tensor", tensor, sems, rng, drains, collect)

            @block.vector
            def _(vector):
                B.run("vector", vector, sems, rng, drains, collect)

            @block.scalar
            def _(scalar):
                B.run("scalar", scalar, sems, rng, drains, collect)

    return nc


_CACHE = {}


def _host_prep(inputs):
    x = np.ascontiguousarray(np.asarray(inputs["L0_attn_hidden_states"]))

    def gu_tiles(w):
        t = np.asarray(w).reshape(KIN, 128, HCN, 128).transpose(0, 3, 2, 1)
        return np.ascontiguousarray(t.reshape(KIN * 128, D).astype(np.int32))

    wgT = gu_tiles(inputs["L0_ffn_gate_w"])
    wuT = gu_tiles(inputs["L0_ffn_up_w"])
    t = np.asarray(inputs["L0_ffn_down_w"]).reshape(4, 512, KIN, 128).transpose(0, 2, 3, 1)
    wdT = np.ascontiguousarray(t.reshape(4 * KIN * 128, 512).astype(np.int32))

    wixT = np.ascontiguousarray(np.asarray(inputs["L0_attn_W_in_x"]).T.astype(np.float32))
    wizT = np.ascontiguousarray(np.asarray(inputs["L0_attn_W_in_z"]).T.astype(np.float32))
    wpT = np.ascontiguousarray(np.asarray(inputs["L0_attn_W_param"]).T.astype(np.float32))
    wdtT = np.ascontiguousarray(np.asarray(inputs["L0_attn_W_dt"]).T.astype(np.float32))
    woT = np.ascontiguousarray(np.asarray(inputs["L0_attn_W_out"]).T.astype(np.float32))

    ebr = np.zeros((NH, CI), np.float32)
    ebr[np.arange(CI) // 16, np.arange(CI)] = 1.0
    ident = np.eye(128, dtype=ml_dtypes.bfloat16)

    ppc = np.zeros((128, 158), np.float32)
    ppc[:, ALOG:ALOG + 4] = np.asarray(inputs["L0_attn_A_log"]).astype(np.float32)\
        .reshape(CI).reshape(CCN, 128).T
    ppc[:, CONVW:CONVW + 16] = np.asarray(inputs["L0_attn_conv_w"]).astype(np.float32)\
        .reshape(CI, 4).reshape(CCN, 128, 4).transpose(1, 0, 2).reshape(128, 16)
    ppc[:, CONVB:CONVB + 4] = np.asarray(inputs["L0_attn_conv_b"]).astype(np.float32)\
        .reshape(CCN, 128).T
    ppc[:, DVEC:DVEC + 4] = np.asarray(inputs["L0_attn_D"]).astype(np.float32)\
        .reshape(CCN, 128).T
    ppc[0:NH, BDT] = np.asarray(inputs["L0_attn_b_dt"]).astype(np.float32)
    ppc[:, GSC:GSC + 64] = np.asarray(inputs["L0_ffn_gate_scale"]).astype(np.float32)\
        .reshape(KIN, 128).T
    ppc[:, USC:USC + 64] = np.asarray(inputs["L0_ffn_up_scale"]).astype(np.float32)\
        .reshape(KIN, 128).T

    dscale = np.ascontiguousarray(np.broadcast_to(
        np.asarray(inputs["L0_ffn_down_scale"]).astype(np.float32)[None, :],
        (128, D)))

    shared = dict(wgT=wgT, wuT=wuT, wdT=wdT, wixT=wixT, wizT=wizT, wpT=wpT,
                  wdtT=wdtT, woT=woT, ebr=ebr, ident=ident, dscale=dscale)

    in_maps = []
    for c in range(8):
        b, half = c // 2, c % 2
        if half == 1:
            xl = x[b]
        else:
            xl = np.concatenate([x[b, 1024:], x[b, :1024]], axis=0)
        ppc_c = ppc.copy()
        ppc_c[:, PMASK] = float(half)
        m = dict(shared)
        m["x"] = np.ascontiguousarray(xl)
        m["ppc"] = ppc_c
        in_maps.append(m)
    return in_maps


def kernel(**inputs):
    if "nc" not in _CACHE:
        _CACHE["nc"] = build_program()
    nc = _CACHE["nc"]
    in_maps = _host_prep(inputs)
    res = run_bass_kernel_spmd(nc, in_maps, core_ids=list(range(8)))
    out = np.empty((4, 2048, 2048), np.float32)
    for c in range(8):
        b, half = c // 2, c % 2
        out[b, half * 1024:(half + 1) * 1024] = res.results[c]["out"]
    return out
